# version 09
# baseline (speedup 1.0000x reference)
import numpy as np
from contextlib import ExitStack

import concourse.bass as bass
import concourse.mybir as mybir
from concourse.bass_utils import run_bass_kernel_spmd

F32 = mybir.dt.float32
BF16 = mybir.dt.bfloat16
AF = mybir.ActivationFunctionType
ALU = mybir.AluOpType
AX = mybir.AxisListType.X

H = 4096
T = 1024
NCH = 32
HEADS = 32
HD = 128
SCALE = 1.0 / float(np.sqrt(HD))
CEXP = 30.0
VIS_N = 614
TEX_N = 2048 - VIS_N
EPS = 1e-5

PH_ALL = ["A", "B", "C", "D", "D5", "E", "F"]
D5_ILV = 16


def build(upto="F", debug=False):
    phases = PH_ALL[: PH_ALL.index(upto) + 1]

    def ph(p):
        return p in phases

    nc = bass.Bass("TRN2", target_bir_lowering=False, debug=False)

    x_in = nc.dram_tensor("x_in", [H, T], F32, kind="ExternalInput").ap()
    qkvw_t = nc.dram_tensor("qkvw_t", [96, 128, H], F32, kind="ExternalInput").ap()
    ow_t = nc.dram_tensor("ow_t", [NCH, 128, H], F32, kind="ExternalInput").ap()
    w1_t = nc.dram_tensor("w1_t", [3, NCH, 128, H], F32, kind="ExternalInput").ap()
    w2_t = nc.dram_tensor("w2_t", [3, NCH, 128, H], F32, kind="ExternalInput").ap()
    qkvb_t = nc.dram_tensor("qkvb_t", [128, 96], F32, kind="ExternalInput").ap()
    ob_t = nc.dram_tensor("ob_t", [128, NCH], F32, kind="ExternalInput").ap()
    b1_t = nc.dram_tensor("b1_t", [128, 3 * NCH], F32, kind="ExternalInput").ap()
    b2_t = nc.dram_tensor("b2_t", [128, 3 * NCH], F32, kind="ExternalInput").ap()
    gww_t = nc.dram_tensor("gww_t", [128, 3 * NCH], F32, kind="ExternalInput").ap()
    gl_gb = nc.dram_tensor("gl_gb", [1, 8], F32, kind="ExternalInput").ap()
    mmask_in = nc.dram_tensor("mmask_in", [1, 8], F32, kind="ExternalInput").ap()
    ident_in = nc.dram_tensor("ident_in", [128, 128], F32, kind="ExternalInput").ap()

    y_out = nc.dram_tensor("y_out", [H, T], F32, kind="ExternalOutput").ap()

    q_spill = nc.dram_tensor("q_spill", [H, T], BF16, kind="Internal").ap()
    kv_contrib = nc.dram_tensor("kv_contrib", [2 * H, T], BF16, kind="Internal").ap()
    kv_gath = nc.dram_tensor("kv_gath", [4 * H, T], BF16, kind="Internal").ap()
    x1_spill = nc.dram_tensor("x1_spill", [H, T], F32, kind="Internal").ap()
    acc_dram = nc.dram_tensor("acc_dram", [H, T], BF16, kind="Internal").ap()
    ct_dram = nc.dram_tensor("ct_dram", [3, 8 * 128], BF16, kind="Internal").ap()
    st_in = nc.dram_tensor("st_in", [1, 8], F32, kind="Internal").ap()
    st_out = nc.dram_tensor("st_out", [1, 8], F32, kind="Internal").ap()
    ctx_snap = nc.dram_tensor("ctx_snap", [128, NCH * T], BF16, kind="Internal").ap() \
        if debug else None

    dbg = {}
    if debug:
        for nm, shape in [
            ("dbg_q", [H, T]), ("dbg_kvg", [4 * H, T]), ("dbg_x1", [H, T]),
            ("dbg_acc", [H, T]), ("dbg_h", [128, NCH * T]), ("dbg_ct", [3, 8 * 128]),
            ("dbg_st", [1, 8]), ("dbg_cn", [1, 64]), ("dbg_ctx", [128, NCH * T]),
            ("dbg_xh", [128, NCH * T]),
        ]:
            dbg[nm] = nc.dram_tensor(nm, shape, F32, kind="ExternalOutput").ap()

    with ExitStack() as ctx:
        sb = lambda n, sh, d: ctx.enter_context(nc.sbuf_tensor(n, sh, d))
        psb = lambda n, sh, d: ctx.enter_context(nc.psum_tensor(n, sh, d))
        sem = lambda n: ctx.enter_context(nc.semaphore(n))

        A1 = sb("A1", [128, NCH * T], BF16)
        A2 = sb("A2", [128, NCH * T], BF16)
        WS = sb("WS", [128, 2 * H], BF16)
        XS = sb("XS", [128, 3 * T], F32)
        YT = sb("YT", [128, 3 * T], F32)
        ES = sb("ES", [128, 2 * T], BF16)
        ACS = sb("ACS", [128, 2 * T], BF16)
        SQ = sb("SQ", [128, 2 * T], BF16)
        BCA = sb("BCA", [128, T], F32)
        BCB = sb("BCB", [128, T], F32)
        R = sb("R", [1, 3 * T], F32)
        CROW = sb("CROW", [1, T], BF16)
        GT = sb("GT", [128, 32], F32)
        GT2 = sb("GT2", [128, 32], F32)
        GE = sb("GE", [128, 32], F32)
        NM = sb("NM", [128, 8], F32)
        GSC = sb("GSC", [128, 16], F32)
        SM3 = sb("SM3", [128, 16], F32)
        CTB = sb("CTB", [128, 24], BF16)
        CTT = sb("CTT", [24, 128], BF16)
        SM = sb("SM", [1, 64], F32)
        ONESB = sb("ONESB", [128, 1], BF16)
        CEXPB = sb("CEXPB", [128, 1], F32)
        ONESBF = sb("ONESBF", [128, 1], F32)
        ONESR = sb("ONESR", [1, 128], BF16)
        ONESRF = sb("ONESRF", [1, 128], F32)
        IDENB = sb("IDENB", [128, 128], BF16)
        QKVB = sb("QKVB", [128, 96], F32)
        QKVBS = sb("QKVBS", [128, NCH], F32)
        OBT = sb("OBT", [128, NCH], F32)
        B1T = sb("B1T", [128, 3 * NCH], F32)
        B2T = sb("B2T", [128, 3 * NCH], F32)
        GWWB = sb("GWWB", [128, 3 * NCH], BF16)
        GLGB = sb("GLGB", [1, 8], F32)
        MMSK = sb("MMSK", [1, 8], F32)
        CONTR = sb("CONTR", [1, 8], F32)
        STAT = sb("STAT", [1, 8], F32)

        R_M, R_T1, R_V = 0 * T, 1 * T, 2 * T
        R_RQ = R_V
        R_L, R_LI = R_M, R_V
        R_ABS = R_M

        LACB = YT[:, 0:512].bitcast(BF16)
        LCLB = YT[:, 512:1536]
        R3 = YT[:, 2048:2560].bitcast(BF16)[0:3, :]
        CEB = XS[:, 0:1536].bitcast(BF16)

        PT = A2[:, 0:16 * T]
        C0 = 16 * T
        KHB = A2[:, C0:C0 + 4096]
        VHB = A2[:, C0 + 4096:C0 + 8192]
        VTK = A2[:, C0 + 8192:C0 + 12288]
        QHB = A2[:, C0 + 12288:C0 + 14336]

        PA = [psb(f"PA{j}", [128, 1024], F32) for j in range(4)]
        P = [PA[i // 2][:, (i % 2) * 512:(i % 2 + 1) * 512] for i in range(8)]
        Pb = [p.bitcast(BF16) for p in P]

        names = [
            "pre", "preg", "preV",
            "aX0", "aX1", "aX2", "aSQ", "aSTc", "aXH", "aCP",
            "bW0", "bW1", "bMM", "bEV", "bST0", "bST1", "bCC",
            "cK0", "cK1", "cV0", "cV1", "cQ0", "cQ1",
            "cTR", "cTRV", "cS", "cE", "cL", "cLCV",
            "cLM", "cLE", "cLI", "cLB", "cLBE", "cCX", "cCV",
            "dW0", "dW1", "dMM", "dEV", "dSQ", "dSTc",
            "dX0", "dX1_", "dX2", "dX1", "dX1S0", "dX1S1", "dX1S2",
            "dSQ3", "dST3c", "LNR", "LNW", "LNB", "LNE",
            "eH", "eHA", "eGMc", "eROW", "eGT",
            "eMSK", "eCW", "eCC", "eSL",
            "eB3", "eB3E", "eGP", "eGE", "eCT", "eGTE", "eCTE", "eCTW", "eCRR", "eCB", "eCBE",
            "fW10", "fW11", "fM1", "fE1", "fM2",
            "fAL0", "fAL1", "fAEV", "fAS0", "fAS1",
            "fSQ4", "fST4c",
            "gA0", "gA1", "gX10", "gX11", "gX12", "gY", "gOUT", "dbgS", "vSF",
        ]
        s = {n: sem(n) for n in names}
        aXs = [s["aX0"], s["aX1"], s["aX2"]]
        bWs = [s["bW0"], s["bW1"]]
        bSTs = [s["bST0"], s["bST1"]]
        cKs = [s["cK0"], s["cK1"]]
        cVs = [s["cV0"], s["cV1"]]
        cQs = [s["cQ0"], s["cQ1"]]
        dWs = [s["dW0"], s["dW1"]]
        dXs = [s["dX0"], s["dX1_"], s["dX2"]]
        dX1Ss = [s["dX1S0"], s["dX1S1"], s["dX1S2"]]
        fW1s = [s["fW10"], s["fW11"]]
        fALs = [s["fAL0"], s["fAL1"]]
        fASs = [s["fAS0"], s["fAS1"]]
        gAs = [s["gA0"], s["gA1"]]
        gX1s = [s["gX10"], s["gX11"], s["gX12"]]

        block = ctx.enter_context(nc.Block())

        fo_order = list(range(32, 96)) + list(range(32))

        esched = []
        if ph("E"):
            for e in range(3):
                for fo in range(NCH):
                    esched.append((w1_t[e, fo], "fM1", e * NCH + fo + 1))
                for fo in range(NCH):
                    esched.append((w2_t[e, fo], "fM2", e * NCH + fo + 1))

        fins = {
            "A": [(s["aXH"], NCH)],
            "B": [(bSTs[0], 16 * 48), (bSTs[1], 16 * 48)],
            "C": [(s["cCV"], HEADS)],
            "D": [(s["LNE"], 3)],
            "D5": [(s["eCBE"], 3)],
            "E": [(fASs[0], 16 * 48), (fASs[1], 16 * 48)],
            "F": [(s["gOUT"], 16 * NCH)],
        }[upto]

        @block.gpsimd
        def _(g):
            g.dma_start(out=IDENB[:, :], in_=ident_in).then_inc(s["preg"], 16)
            g.dma_start(out=GWWB[:, :], in_=gww_t).then_inc(s["preg"], 16)
            if ph("A"):
                for c in range(1, NCH, 2):
                    if c >= 3:
                        g.wait_ge(s["aSQ"], c - 2)
                        g.wait_ge(s["aSTc"], c - 2)
                        g.wait_ge(s["aCP"], c - 2)
                    g.dma_start(out=XS[:, (c % 3) * T:(c % 3 + 1) * T],
                                in_=x_in[c * 128:(c + 1) * 128, :]).then_inc(aXs[c % 3], 16)
            if ph("B"):
                for i, fo in enumerate(fo_order):
                    if i == 64:
                        g.wait_ge(bSTs[0], 16 * 32)
                        g.wait_ge(bSTs[1], 16 * 32)
                        g.collective_compute(
                            "AllGather", ALU.bypass,
                            replica_groups=[[0, 1], [2, 3], [4, 5], [6, 7]],
                            ins=[kv_contrib], outs=[kv_gath],
                        ).then_inc(s["bCC"], 1)
                    if i >= 2:
                        g.wait_ge(s["bMM"], i - 1)
                    g.dma_start(out=WS[:, (i % 2) * H:(i % 2 + 1) * H],
                                in_=qkvw_t[fo]).then_inc(bWs[i % 2], 16)
            if ph("D"):
                for fo in range(NCH):
                    if fo < 2:
                        g.wait_ge(s["bMM"], 96)
                    else:
                        g.wait_ge(s["dMM"], fo - 1)
                    g.dma_start(out=WS[:, (fo % 2) * H:(fo % 2 + 1) * H],
                                in_=ow_t[fo]).then_inc(dWs[fo % 2], 16)
            if debug and ph("C"):
                g.wait_ge(s["cCV"], HEADS)
                g.dma_start(out=ctx_snap, in_=A1[:, :]).then_inc(s["dbgS"], 16)
            if ph("D5"):
                g.wait_ge(s["eCW"], 16)
                g.collective_compute(
                    "AllReduce", ALU.add,
                    replica_groups=[[0, 1], [2, 3], [4, 5], [6, 7]],
                    ins=[st_in], outs=[st_out],
                ).then_inc(s["eCC"], 1)
            if ph("E"):
                for j, (src, csem, ccnt) in enumerate(esched):
                    if j >= 2:
                        psem, pcnt = esched[j - 2][1], esched[j - 2][2]
                        g.wait_ge(s[psem], pcnt)
                    g.dma_start(out=WS[:, (j % 2) * H:(j % 2 + 1) * H],
                                in_=src).then_inc(fW1s[j % 2], 16)
            if ph("F"):
                g.wait_ge(s["eGT"], 1)
                g.wait_ge(s["eH"], NCH)
                for c in range(1, NCH, 2):
                    if c >= 3:
                        g.wait_ge(s["gY"], c - 1)
                        g.dma_start(out=y_out[(c - 2) * 128:(c - 1) * 128, :],
                                    in_=YT[:, T:2 * T]).then_inc(s["gOUT"], 16)
                    g.dma_start(out=YT[:, T:2 * T],
                                in_=x1_spill[c * 128:(c + 1) * 128, :]
                                ).then_inc(gX1s[1], 16)
                g.wait_ge(s["gY"], NCH)
                g.dma_start(out=y_out[(NCH - 1) * 128:NCH * 128, :],
                            in_=YT[:, T:2 * T]).then_inc(s["gOUT"], 16)
            if debug:
                for fs, fc in fins:
                    g.wait_ge(fs, fc)
                ndbg = 0
                if ph("A") and not ph("C"):
                    g.dma_start(out=dbg["dbg_xh"], in_=A2[:, 0:NCH * T]
                                ).then_inc(s["dbgS"], 16)
                    ndbg += 1
                if ph("B"):
                    g.wait_ge(s["bCC"], 1)
                    g.dma_start(out=dbg["dbg_q"], in_=q_spill).then_inc(s["dbgS"], 16)
                    g.dma_start(out=dbg["dbg_kvg"], in_=kv_gath).then_inc(s["dbgS"], 16)
                    ndbg += 2
                if ph("C"):
                    ndbg += 1
                    g.dma_start(out=dbg["dbg_ctx"], in_=ctx_snap).then_inc(s["dbgS"], 16)
                    ndbg += 1
                if ph("D"):
                    for sl in range(3):
                        n_sl = len([c for c in range(NCH) if c % 3 == sl])
                        g.wait_ge(dX1Ss[sl], 16 * n_sl)
                    g.dma_start(out=dbg["dbg_x1"], in_=x1_spill).then_inc(s["dbgS"], 16)
                    ndbg += 1
                if ph("D5") or ph("D"):
                    g.dma_start(out=dbg["dbg_h"], in_=A2[:, 0:NCH * T]).then_inc(s["dbgS"], 16)
                    ndbg += 1
                if ph("D5"):
                    g.dma_start(out=dbg["dbg_ct"], in_=ct_dram).then_inc(s["dbgS"], 16)
                    g.dma_start(out=dbg["dbg_st"], in_=st_out).then_inc(s["dbgS"], 16)
                    g.dma_start(out=dbg["dbg_cn"], in_=SM[:, :]).then_inc(s["dbgS"], 16)
                    ndbg += 3
                if ph("E"):
                    g.dma_start(out=dbg["dbg_acc"], in_=acc_dram).then_inc(s["dbgS"], 16)
                    ndbg += 1
                g.wait_ge(s["dbgS"], 16 * ndbg)

        @block.sync
        def _(sy):
            for ap_dst, ap_src in [
                (QKVB[:, :], qkvb_t), (OBT[:, :], ob_t), (B1T[:, :], b1_t),
                (B2T[:, :], b2_t), (GLGB[:, :], gl_gb), (MMSK[:, :], mmask_in),
            ]:
                sy.dma_start(out=ap_dst, in_=ap_src).then_inc(s["pre"], 16)
            if ph("A"):
                for c in range(0, NCH, 2):
                    if c >= 3:
                        sy.wait_ge(s["aSQ"], c - 2)
                        sy.wait_ge(s["aSTc"], c - 2)
                        sy.wait_ge(s["aCP"], c - 2)
                    sy.dma_start(out=XS[:, (c % 3) * T:(c % 3 + 1) * T],
                                 in_=x_in[c * 128:(c + 1) * 128, :]).then_inc(aXs[c % 3], 16)
            if ph("B"):
                for i, fo in enumerate(fo_order):
                    sy.wait_ge(s["bEV"], i + 1)
                    tile = ES[:, (i % 2) * T:(i % 2 + 1) * T]
                    if fo < 32:
                        sy.dma_start(out=q_spill[fo * 128:(fo + 1) * 128, :],
                                     in_=tile).then_inc(bSTs[i % 2], 16)
                    else:
                        sy.dma_start(out=kv_contrib[(fo - 32) * 128:(fo - 31) * 128, :],
                                     in_=tile).then_inc(bSTs[i % 2], 16)
            if ph("C"):
                for h in range(HEADS):
                    hb = h % 2
                    if h == 0:
                        sy.wait_ge(s["bCC"], 1)
                        sy.wait_ge(s["bMM"], 96)
                    if h >= 2:
                        sy.wait_ge(s["cS"], 16 * (h - 1))
                        sy.wait_ge(s["cTR"], h - 1)
                    for r in range(2):
                        sy.dma_start(
                            out=KHB[:, hb * 2048 + r * T:hb * 2048 + (r + 1) * T],
                            in_=kv_gath[r * 2 * H + h * 128:r * 2 * H + (h + 1) * 128, :]
                        ).then_inc(cKs[hb], 16)
                        sy.dma_start(
                            out=VHB[:, hb * 2048 + r * T:hb * 2048 + (r + 1) * T],
                            in_=kv_gath[r * 2 * H + H + h * 128:r * 2 * H + H + (h + 1) * 128, :]
                        ).then_inc(cVs[hb], 16)
                    qi = 64 + h
                    sy.wait_ge(bSTs[qi % 2], 16 * (qi // 2 + 1))
                    sy.dma_start(out=QHB[:, hb * T:(hb + 1) * T],
                                 in_=q_spill[h * 128:(h + 1) * 128, :]).then_inc(cQs[hb], 16)
            if ph("D"):
                for c in range(NCH):
                    if c == 0:
                        sy.wait_ge(s["aSTc"], NCH)
                        sy.wait_ge(s["aCP"], NCH)
                        sy.wait_ge(s["aSQ"], NCH)
                    sy.dma_start(out=XS[:, (c % 3) * T:(c % 3 + 1) * T],
                                 in_=x_in[c * 128:(c + 1) * 128, :]).then_inc(dXs[c % 3], 16)
                    if c >= 1:
                        cs = c - 1
                        sy.wait_ge(s["dX1"], cs + 1)
                        sy.dma_start(out=x1_spill[cs * 128:(cs + 1) * 128, :],
                                     in_=YT[:, (cs % 3) * T:(cs % 3 + 1) * T]
                                     ).then_inc(dX1Ss[cs % 3], 16)
                sy.wait_ge(s["dX1"], NCH)
                sy.dma_start(out=x1_spill[(NCH - 1) * 128:NCH * 128, :],
                             in_=YT[:, ((NCH - 1) % 3) * T:((NCH - 1) % 3 + 1) * T]
                             ).then_inc(dX1Ss[(NCH - 1) % 3], 16)
            if ph("D5"):
                sy.wait_ge(s["eMSK"], 1)
                sy.dma_start(out=st_in, in_=CONTR[:, :]).then_inc(s["eCW"], 16)
                sy.wait_ge(s["eCC"], 1)
                sy.dma_start(out=STAT[:, :], in_=st_out).then_inc(s["eSL"], 16)
                sy.wait_ge(s["eCTE"], 1)
                sy.dma_start(out=ct_dram.rearrange("e (t p) -> (e t) p", p=128),
                             in_=CTT[:, :]).then_inc(s["eCTW"], 16)
                sy.wait_ge(s["eCTW"], 16)
                for e in range(3):
                    if e >= 1:
                        sy.wait_ge(s["eCB"], e)
                    sy.dma_start(out=CROW[:, :],
                                 in_=ct_dram[e:e + 1, :]).then_inc(s["eCRR"], 16)
            if ph("E"):
                for e in range(3):
                    for fo in range(NCH):
                        if e > 0:
                            ip = (e - 1) * NCH + fo
                            sy.wait_ge(fASs[ip % 2], 16 * (ip // 2 + 1))
                            if fo >= 2:
                                sy.wait_ge(s["fAEV"], e * NCH + fo - 1)
                            sy.dma_start(out=ACS[:, (fo % 2) * T:(fo % 2 + 1) * T],
                                         in_=acc_dram[fo * 128:(fo + 1) * 128, :]
                                         ).then_inc(fALs[fo % 2], 16)
                        if fo >= 1:
                            fs = fo - 1
                            sy.wait_ge(s["fAEV"], e * NCH + fs + 1)
                            sy.dma_start(out=acc_dram[fs * 128:(fs + 1) * 128, :],
                                         in_=ES[:, (fs % 2) * T:(fs % 2 + 1) * T]
                                         ).then_inc(fASs[fs % 2], 16)
                    fs = NCH - 1
                    sy.wait_ge(s["fAEV"], e * NCH + fs + 1)
                    sy.dma_start(out=acc_dram[fs * 128:(fs + 1) * 128, :],
                                 in_=ES[:, (fs % 2) * T:(fs % 2 + 1) * T]
                                 ).then_inc(fASs[fs % 2], 16)
            if ph("F"):
                for c in range(0, NCH, 2):
                    if c == 0:
                        sy.wait_ge(s["eGT"], 1)
                        sy.wait_ge(s["eH"], NCH)
                    if c >= 2:
                        sy.wait_ge(s["gY"], c - 1)
                        sy.dma_start(out=y_out[(c - 2) * 128:(c - 1) * 128, :],
                                     in_=YT[:, 0:T]).then_inc(s["gOUT"], 16)
                    sy.dma_start(out=YT[:, 0:T],
                                 in_=x1_spill[c * 128:(c + 1) * 128, :]
                                 ).then_inc(gX1s[0], 16)
                sy.wait_ge(s["gY"], NCH - 1)
                sy.dma_start(out=y_out[(NCH - 2) * 128:(NCH - 1) * 128, :],
                             in_=YT[:, 0:T]).then_inc(s["gOUT"], 16)
            for fs, fc in fins:
                sy.wait_ge(fs, fc)

        @block.tensor
        def _(t):
            t.wait_ge(s["preg"], 32)
            t.wait_ge(s["preV"], 1)

            def stats4(c, src_f32, src_bf, sq, b1, b2, w1s, w1c, w2s, w2c, tagc):
                t.wait_ge(s[w1s], w1c)
                t.wait_ge(s[w2s], w2c)
                for hh in range(2):
                    if src_f32 is not None:
                        t.matmul(P[b1 + hh][0:1, :], ONESBF[:, :],
                                 src_f32[:, hh * 512:(hh + 1) * 512],
                                 start=(c == 0), stop=(c == NCH - 1))
                    else:
                        t.matmul(P[b1 + hh][0:1, :], ONESB[:, :],
                                 src_bf[:, hh * 512:(hh + 1) * 512],
                                 start=(c == 0), stop=(c == NCH - 1))
                mm = None
                for hh in range(2):
                    mm = t.matmul(P[b2 + hh][0:1, :], ONESB[:, :],
                                  sq[:, hh * 512:(hh + 1) * 512],
                                  start=(c == 0), stop=(c == NCH - 1))
                mm.then_inc(s[tagc], 1)

            def bcast(b1, b2, wsem, wcnt, tag):
                t.wait_ge(s[wsem], wcnt)
                for hh in range(2):
                    t.matmul(P[b1 + hh][:, :], ONESRF[:, :],
                             R[0:1, R_M + hh * 512:R_M + (hh + 1) * 512],
                             start=True, stop=True)
                mm = None
                for hh in range(2):
                    mm = t.matmul(P[b2 + hh][:, :], ONESRF[:, :],
                                  R[0:1, R_RQ + hh * 512:R_RQ + (hh + 1) * 512],
                                  start=True, stop=True)
                mm.then_inc(s[tag], 1)

            if ph("A"):
                for c in range(NCH):
                    stats4(c, XS[:, (c % 3) * T:(c % 3 + 1) * T], None,
                           SQ[:, (c % 2) * T:(c % 2 + 1) * T],
                           0, 2, f"aX{c % 3}", 16 * (c // 3 + 1), "aSQ", c + 1, "aSTc")
                bcast(4, 6, "LNW", 1, "LNB")

            if ph("B"):
                for i, fo in enumerate(fo_order):
                    t.wait_ge(bWs[i % 2], 16 * (i // 2 + 1))
                    if i == 0:
                        t.wait_ge(s["aXH"], NCH)
                    if i >= 2:
                        t.wait_ge(s["bEV"], i - 1)
                    pb = 0 if i % 2 == 0 else 2
                    w = WS[:, (i % 2) * H:(i % 2 + 1) * H]
                    mm = None
                    for fin in range(NCH):
                        lhs = w[:, fin * 128:(fin + 1) * 128]
                        for hh in range(2):
                            mm = t.matmul(P[pb + hh][:, :], lhs,
                                          A2[:, fin * T + hh * 512:fin * T + (hh + 1) * 512],
                                          start=(fin == 0), stop=(fin == NCH - 1))
                    mm.then_inc(s["bMM"], 1)

            if ph("C"):
                for h in range(HEADS):
                    hb = h % 2
                    t.wait_ge(cVs[hb], 32 * (h // 2 + 1))
                    if h >= 1:
                        t.wait_ge(s["cLBE"], h)
                    if h >= 2:
                        t.wait_ge(s["cCX"], h - 1)
                    mm = None
                    for k in range(16):
                        bank = 6 + (k // 8)
                        off = (k % 8) * 128
                        mm = t.transpose(Pb[bank][:, off:off + 128],
                                         VHB[:, hb * 2048 + k * 128:hb * 2048 + (k + 1) * 128],
                                         IDENB[:, :])
                    mm.then_inc(s["cTR"], 1)
                    t.wait_ge(cKs[hb], 32 * (h // 2 + 1))
                    t.wait_ge(cQs[hb], 16 * (h // 2 + 1))
                    for k in range(16):
                        pib = 0 if k % 2 == 0 else 2
                        t.wait_ge(s["cE"], max(0, 16 * h + k - 1))
                        mm = None
                        for hh in range(2):
                            mm = t.matmul(P[pib + hh][:, :],
                                          KHB[:, hb * 2048 + k * 128:hb * 2048 + (k + 1) * 128],
                                          QHB[:, hb * T + hh * 512:hb * T + (hh + 1) * 512],
                                          start=True, stop=True)
                        mm.then_inc(s["cS"], 1)
                        if k >= 1:
                            kc = k - 1
                            t.wait_ge(s["cE"], 16 * h + kc + 1)
                            if kc == 0:
                                t.wait_ge(s["cCV"], h)
                                t.wait_ge(s["cTRV"], h + 1)
                            for hh in range(2):
                                t.matmul(P[4 + hh][:, :],
                                         VTK[:, hb * 2048 + kc * 128:hb * 2048 + (kc + 1) * 128],
                                         PT[:, kc * T + hh * 512:kc * T + (hh + 1) * 512],
                                         start=(kc == 0), stop=False)
                    t.wait_ge(s["cE"], 16 * h + 16)
                    mm = None
                    for hh in range(2):
                        mm = t.matmul(P[4 + hh][:, :],
                                      VTK[:, hb * 2048 + 15 * 128:hb * 2048 + 16 * 128],
                                      PT[:, 15 * T + hh * 512:15 * T + (hh + 1) * 512],
                                      start=False, stop=True)
                    mm.then_inc(s["cCX"], 1)
                    t.wait_ge(s["cLCV"], h + 1)
                    t.wait_ge(s["cTRV"], h + 1)
                    mm = None
                    for hh in range(2):
                        mm = t.matmul(P[6 + hh][0:1, :], ONESB[:, :],
                                      LACB[:, hh * 512:(hh + 1) * 512], start=True, stop=True)
                    mm.then_inc(s["cLM"], 1)
                    t.wait_ge(s["cLI"], h + 1)
                    mm = None
                    for hh in range(2):
                        mm = t.matmul(P[6 + hh][:, :], ONESRF[:, :],
                                      R[0:1, R_LI + hh * 512:R_LI + (hh + 1) * 512],
                                      start=True, stop=True)
                    mm.then_inc(s["cLB"], 1)

            if ph("D"):
                for fo in range(NCH):
                    t.wait_ge(dWs[fo % 2], 16 * (fo // 2 + 1))
                    if fo == 0:
                        t.wait_ge(s["cCV"], HEADS)
                    if fo >= 2:
                        t.wait_ge(s["dEV"], fo - 1)
                    pb = 0 if fo % 2 == 0 else 2
                    w = WS[:, (fo % 2) * H:(fo % 2 + 1) * H]
                    mm = None
                    for fin in range(NCH):
                        lhs = w[:, fin * 128:(fin + 1) * 128]
                        for hh in range(2):
                            mm = t.matmul(P[pb + hh][:, :], lhs,
                                          A1[:, fin * T + hh * 512:fin * T + (hh + 1) * 512],
                                          start=(fin == 0), stop=(fin == NCH - 1))
                    mm.then_inc(s["dMM"], 1)
                    if fo >= 2:
                        c = fo - 2
                        stats4(c, None, A2[:, c * T:(c + 1) * T],
                               SQ[:, (c % 2) * T:(c % 2 + 1) * T],
                               4, 6, "dEV", c + 1, "dSQ", c + 1, "dSTc")
                for c in [NCH - 2, NCH - 1]:
                    stats4(c, None, A2[:, c * T:(c + 1) * T],
                           SQ[:, (c % 2) * T:(c % 2 + 1) * T],
                           4, 6, "dEV", c + 1, "dSQ", c + 1, "dSTc")
                bcast(4, 6, "LNW", 2, "LNB")
                for c in range(NCH):
                    stats4(c, YT[:, (c % 3) * T:(c % 3 + 1) * T], None,
                           SQ[:, (c % 2) * T:(c % 2 + 1) * T],
                           0, 2, "dX1", c + 1, "dSQ3", c + 1, "dST3c")
                bcast(0, 2, "LNW", 3, "LNB")

            if ph("D5"):
                for c in range(NCH):
                    t.wait_ge(s["eHA"], c + 1)
                    for hh in range(2):
                        t.matmul(P[4 + hh][0:1, :], ONESB[:, :],
                                 ES[:, (c % 2) * T + hh * 512:(c % 2) * T + (hh + 1) * 512],
                                 start=(c == 0), stop=(c == NCH - 1))
                    mm = None
                    for hh in range(2):
                        mm = t.matmul(P[6 + hh][0:3, :], GWWB[:, c * 3:(c + 1) * 3],
                                      A2[:, c * T + hh * 512:c * T + (hh + 1) * 512],
                                      start=(c == 0), stop=(c == NCH - 1))
                    mm.then_inc(s["eGMc"], 1)
                t.wait_ge(s["eROW"], 1)
                mm = None
                for tt in range(8):
                    mm = t.transpose(Pb[4][:, tt * 4:tt * 4 + 3],
                                     R3[:, tt * 128:(tt + 1) * 128], IDENB[0:3, 0:3])
                mm.then_inc(s["eGT"], 1)

            def d5_tail(t):
                t.wait_ge(s["eB3"], 1)
                t.matmul(P[5][:, 0:3], ONESRF[:, :], SM[0:1, 8:11], start=True, stop=True)
                t.matmul(P[6][:, 0:3], ONESRF[:, :], SM[0:1, 16:19], start=True, stop=True)
                mm = t.matmul(P[7][:, 0:3], ONESRF[:, :], SM[0:1, 24:27], start=True, stop=True)
                mm.then_inc(s["eB3E"], 1)
                t.wait_ge(s["eCT"], 1)
                mm = t.transpose(Pb[5][0:24, 0:128], CTB[:, :], IDENB[:, :])
                mm.then_inc(s["eGTE"], 1)
                for e in range(3):
                    t.wait_ge(s["eCRR"], 16 * (e + 1))
                    if e >= 1:
                        t.wait_ge(s["eCBE"], e)
                    mm = None
                    for hh in range(2):
                        mm = t.matmul(P[6 + hh][:, :], ONESR[:, :],
                                      CROW[0:1, hh * 512:(hh + 1) * 512],
                                      start=True, stop=True)
                    mm.then_inc(s["eCB"], 1)

            if ph("D5") and (not ph("E") or D5_ILV == 0):
                d5_tail(t)

            if ph("E"):
                for e in range(3):
                    for fo in range(NCH):
                        i = e * NCH + fo
                        jj = e * 2 * NCH + fo
                        if D5_ILV and i == D5_ILV:
                            d5_tail(t)
                        t.wait_ge(fW1s[jj % 2], 16 * (jj // 2 + 1))
                        if i == 0:
                            t.wait_ge(s["eH"], NCH)
                        if i >= 2:
                            t.wait_ge(s["fE1"], i - 1)
                        if e > 0 and fo == 0:
                            t.wait_ge(s["fM2"], e * NCH)
                        pb = 0 if i % 2 == 0 else 2
                        w = WS[:, ((e * 2 * NCH + fo) % 2) * H:
                               ((e * 2 * NCH + fo) % 2 + 1) * H]
                        mm = None
                        for fin in range(NCH):
                            lhs = w[:, fin * 128:(fin + 1) * 128]
                            for hh in range(2):
                                mm = t.matmul(P[pb + hh][:, :], lhs,
                                              A2[:, fin * T + hh * 512:fin * T + (hh + 1) * 512],
                                              start=(fin == 0), stop=(fin == NCH - 1))
                        mm.then_inc(s["fM1"], 1)
                    for fo in range(NCH):
                        i = e * NCH + fo
                        jj = e * 2 * NCH + NCH + fo
                        t.wait_ge(fW1s[jj % 2], 16 * (jj // 2 + 1))
                        if i == 0:
                            t.wait_ge(s["eCBE"], 3)
                        if fo == 0:
                            t.wait_ge(s["fE1"], e * NCH + NCH)
                        if i >= 2:
                            t.wait_ge(s["fAEV"], i - 1)
                        pb = 4 if i % 2 == 0 else 6
                        w = WS[:, ((e * 2 * NCH + NCH + fo) % 2) * H:
                               ((e * 2 * NCH + NCH + fo) % 2 + 1) * H]
                        mm = None
                        for hh in range(2):
                            for fin in range(NCH):
                                lhs = w[:, fin * 128:(fin + 1) * 128]
                                mm = t.matmul(P[pb + hh][:, :], lhs,
                                              A1[:, fin * T + hh * 512:fin * T + (hh + 1) * 512],
                                              start=(fin == 0), stop=(fin == NCH - 1))
                        mm.then_inc(s["fM2"], 1)
                        if e == 2 and fo >= 2:
                            c = fo - 2
                            stats4(c, None, ES[:, (c % 2) * T:(c % 2 + 1) * T],
                                   SQ[:, (c % 2) * T:(c % 2 + 1) * T],
                                   0, 2, "fAEV", 2 * NCH + c + 1, "fSQ4", c + 1, "fST4c")
                for c in [NCH - 2, NCH - 1]:
                    stats4(c, None, ES[:, (c % 2) * T:(c % 2 + 1) * T],
                           SQ[:, (c % 2) * T:(c % 2 + 1) * T],
                           0, 2, "fAEV", 2 * NCH + c + 1, "fSQ4", c + 1, "fST4c")
                bcast(0, 2, "LNW", 4, "LNB")

        @block.scalar
        def _(sc):
            sc.wait_ge(s["pre"], 96)
            sc.wait_ge(s["preg"], 32)

            def rsqrt_rows(wsem, wcnt, tag):
                sc.wait_ge(s[wsem], wcnt)
                sc.activation(R[0:1, R_T1:R_T1 + T], R[0:1, R_V:R_V + T], AF.Ln)
                sc.activation(R[0:1, R_RQ:R_RQ + T], R[0:1, R_T1:R_T1 + T], AF.Exp,
                              scale=-0.5).then_inc(s[tag], 1)

            if ph("A"):
                for c in range(NCH):
                    sc.wait_ge(aXs[c % 3], 16 * (c // 3 + 1))
                    if c >= 2:
                        sc.wait_ge(s["aSTc"], c - 1)
                    sc.activation(SQ[:, (c % 2) * T:(c % 2 + 1) * T],
                                  XS[:, (c % 3) * T:(c % 3 + 1) * T],
                                  AF.Square).then_inc(s["aSQ"], 1)
                rsqrt_rows("LNR", 1, "LNW")

            if ph("B"):
                sc.wait_ge(s["preV"], 1)
                for i, fo in enumerate(fo_order):
                    sc.wait_ge(s["bMM"], i + 1)
                    if i >= 2:
                        sc.wait_ge(bSTs[i % 2], 16 * ((i - 2) // 2 + 1))
                    pb = 0 if i % 2 == 0 else 2
                    tile = ES[:, (i % 2) * T:(i % 2 + 1) * T]
                    if fo < 32:
                        bias, scl = QKVBS[:, fo:fo + 1], SCALE
                    else:
                        bias, scl = QKVB[:, fo:fo + 1], 1.0
                    a = None
                    for hh in range(2):
                        a = sc.activation(tile[:, hh * 512:(hh + 1) * 512], P[pb + hh][:, :],
                                          AF.Identity, bias=bias, scale=scl)
                    a.then_inc(s["bEV"], 1)

            if ph("C"):
                for h in range(HEADS):
                    for k in range(16):
                        sc.wait_ge(s["cS"], 16 * h + k + 1)
                        a = sc.activation(PT[:, k * T:(k + 1) * T], PA[k % 2][:, :],
                                          AF.Exp, bias=CEXPB[:, :])
                        a.then_inc(s["cE"], 1)
                    if h >= 1:
                        sc.wait_ge(s["cLM"], h)
                    sc.wait_ge(s["cL"], h + 1)
                    sc.activation(LACB[:, :], LCLB[:, :], AF.Copy).then_inc(s["cLCV"], 1)
                    sc.wait_ge(s["cLM"], h + 1)
                    for hh in range(2):
                        sc.activation(R[0:1, R_L + hh * 512:R_L + (hh + 1) * 512],
                                      P[6 + hh][0:1, :], AF.Copy)
                    sc.activation(R[0:1, R_T1:R_T1 + T], R[0:1, R_L:R_L + T], AF.Ln)
                    sc.activation(R[0:1, R_LI:R_LI + T], R[0:1, R_T1:R_T1 + T], AF.Exp,
                                  scale=-1.0).then_inc(s["cLI"], 1)

            if ph("D"):
                for fo in range(NCH):
                    sc.wait_ge(s["dMM"], fo + 1)
                    pb = 0 if fo % 2 == 0 else 2
                    dst = A2[:, fo * T:(fo + 1) * T]
                    a = None
                    for hh in range(2):
                        a = sc.activation(dst[:, hh * 512:(hh + 1) * 512], P[pb + hh][:, :],
                                          AF.Identity, bias=OBT[:, fo:fo + 1])
                    a.then_inc(s["dEV"], 1)
                    if fo >= 2:
                        sc.wait_ge(s["dSTc"], fo - 1)
                    sc.activation(SQ[:, (fo % 2) * T:(fo % 2 + 1) * T], dst,
                                  AF.Square).then_inc(s["dSQ"], 1)
                rsqrt_rows("LNR", 2, "LNW")
                for c in range(NCH):
                    sc.wait_ge(s["dX1"], c + 1)
                    if c >= 2:
                        sc.wait_ge(s["dST3c"], c - 1)
                    sc.activation(SQ[:, (c % 2) * T:(c % 2 + 1) * T],
                                  YT[:, (c % 3) * T:(c % 3 + 1) * T],
                                  AF.Square).then_inc(s["dSQ3"], 1)
                rsqrt_rows("LNR", 3, "LNW")

            if ph("D5"):
                for c in range(NCH):
                    sc.wait_ge(s["eH"], c + 1)
                    if c < 2:
                        sc.wait_ge(s["dST3c"], NCH)
                    else:
                        sc.wait_ge(s["eGMc"], c - 1)
                    sc.activation(ES[:, (c % 2) * T:(c % 2 + 1) * T],
                                  A2[:, c * T:(c + 1) * T], AF.Abs).then_inc(s["eHA"], 1)

            def d5_ge(sc):
                sc.wait_ge(s["eGP"], 1)
                a = None
                for tt in range(8):
                    a = sc.activation(GE[:, tt * 4:tt * 4 + 3], GT2[:, tt * 4:tt * 4 + 3],
                                      AF.Exp, bias=NM[:, tt:tt + 1])
                a.then_inc(s["eGE"], 1)

            if ph("D5") and (not ph("E") or D5_ILV == 0):
                d5_ge(sc)

            if ph("E"):
                act_f = [AF.Tanh, AF.Gelu, AF.Silu]
                for e in range(3):
                    for fo in range(NCH):
                        i = e * NCH + fo
                        if D5_ILV and i == D5_ILV:
                            d5_ge(sc)
                        sc.wait_ge(s["fM1"], i + 1)
                        pb = 0 if i % 2 == 0 else 2
                        dst = A1[:, fo * T:(fo + 1) * T]
                        a = None
                        for hh in range(2):
                            a = sc.activation(dst[:, hh * 512:(hh + 1) * 512], P[pb + hh][:, :],
                                              act_f[e],
                                              bias=B1T[:, e * NCH + fo:e * NCH + fo + 1])
                        a.then_inc(s["fE1"], 1)
                for c in range(NCH):
                    sc.wait_ge(s["fAEV"], 2 * NCH + c + 1)
                    if c >= 2:
                        sc.wait_ge(s["fST4c"], c - 1)
                    sc.activation(SQ[:, (c % 2) * T:(c % 2 + 1) * T],
                                  ES[:, (c % 2) * T:(c % 2 + 1) * T],
                                  AF.Square).then_inc(s["fSQ4"], 1)

            if ph("F"):
                for c in range(2):
                    sc.wait_ge(fASs[c], 16 * 48)
                    sc.dma_start(out=ACS[:, c * T:(c + 1) * T],
                                 in_=acc_dram[c * 128:(c + 1) * 128, :]
                                 ).then_inc(gAs[c], 16)
                rsqrt_rows("LNR", 4, "LNW")
                for c in range(2, NCH):
                    sc.wait_ge(s["gY"], c - 1)
                    sc.dma_start(out=ACS[:, (c % 2) * T:(c % 2 + 1) * T],
                                 in_=acc_dram[c * 128:(c + 1) * 128, :]
                                 ).then_inc(gAs[c % 2], 16)

        @block.vector
        def _(v):
            v.memset(ONESB[:, :], 1.0)
            v.memset(CEXPB[:, :], -CEXP)
            v.memset(ONESBF[:, :], 1.0)
            v.memset(ONESR[:, :], 1.0)
            v.memset(ONESRF[:, :], 1.0)
            v.wait_ge(s["pre"], 96)
            v.tensor_scalar(QKVBS[:, :], QKVB[:, 0:NCH], SCALE, None,
                            ALU.mult).then_inc(s["preV"], 1)

            def rows(b1, b2, wsem, wcnt, tag):
                v.wait_ge(s[wsem], wcnt)
                for hh in range(2):
                    v.tensor_scalar(R[0:1, R_M + hh * 512:R_M + (hh + 1) * 512],
                                    P[b1 + hh][0:1, :], 1.0 / H, None, ALU.mult)
                for hh in range(2):
                    v.tensor_scalar(R[0:1, R_T1 + hh * 512:R_T1 + (hh + 1) * 512],
                                    P[b2 + hh][0:1, :], 1.0 / H, None, ALU.mult)
                v.tensor_mul(R[0:1, R_V:R_V + T], R[0:1, R_M:R_M + T], R[0:1, R_M:R_M + T])
                v.tensor_sub(R[0:1, R_V:R_V + T], R[0:1, R_T1:R_T1 + T], R[0:1, R_V:R_V + T])
                v.tensor_scalar(R[0:1, R_V:R_V + T], R[0:1, R_V:R_V + T], EPS, None,
                                ALU.add).then_inc(s[tag], 1)

            vsf = [0]

            def vbar(ins):
                vsf[0] += 1
                ins.then_inc(s["vSF"], 1)
                v.wait_ge(s["vSF"], vsf[0])

            def bc_evac(b1, b2, wsem, wcnt, tag):
                v.wait_ge(s[wsem], wcnt)
                for hh in range(2):
                    v.tensor_copy(BCA[:, hh * 512:(hh + 1) * 512], P[b1 + hh][:, :])
                cp = None
                for hh in range(2):
                    cp = v.tensor_copy(BCB[:, hh * 512:(hh + 1) * 512], P[b2 + hh][:, :])
                cp.then_inc(s[tag], 1)

            if ph("A"):
                for c in range(NCH):
                    v.wait_ge(aXs[c % 3], 16 * (c // 3 + 1))
                    v.tensor_copy(A1[:, c * T:(c + 1) * T],
                                  XS[:, (c % 3) * T:(c % 3 + 1) * T]).then_inc(s["aCP"], 1)
                rows(0, 2, "aSTc", NCH, "LNR")
                bc_evac(4, 6, "LNB", 1, "LNE")
                for c in range(NCH):
                    tmp = YT[:, (c % 3) * T:(c % 3 + 1) * T]
                    v.tensor_sub(tmp, A1[:, c * T:(c + 1) * T], BCA[:, :])
                    v.tensor_mul(A2[:, c * T:(c + 1) * T], tmp, BCB[:, :]).then_inc(s["aXH"], 1)

            if ph("C"):
                for h in range(HEADS):
                    hb = h % 2
                    v.wait_ge(s["cTR"], h + 1)
                    cp = None
                    for k in range(16):
                        bank = 6 + (k // 8)
                        off = (k % 8) * 128
                        cp = v.tensor_copy(VTK[:, hb * 2048 + k * 128:hb * 2048 + (k + 1) * 128],
                                           Pb[bank][:, off:off + 128])
                    cp.then_inc(s["cTRV"], 1)
                    if h >= 1:
                        v.wait_ge(s["cLCV"], h)
                    cp = None
                    for k in range(16):
                        v.wait_ge(s["cE"], 16 * h + k + 1)
                        if k == 0:
                            cp = v.tensor_copy(LCLB[:, :], PT[:, 0:T])
                        else:
                            cp = v.tensor_add(LCLB[:, :], LCLB[:, :], PT[:, k * T:(k + 1) * T])
                    cp.then_inc(s["cL"], 1)
                    v.wait_ge(s["cLB"], h + 1)
                    cp = None
                    for hh in range(2):
                        cp = v.tensor_copy(LCLB[:, hh * 512:(hh + 1) * 512], P[6 + hh][:, :])
                    cp.then_inc(s["cLBE"], 1)
                    v.wait_ge(s["cCX"], h + 1)
                    cp = None
                    for hh in range(2):
                        cp = v.tensor_tensor(A1[:, h * T + hh * 512:h * T + (hh + 1) * 512],
                                             P[4 + hh][:, :], LCLB[:, hh * 512:(hh + 1) * 512],
                                             ALU.mult)
                    cp.then_inc(s["cCV"], 1)

            if ph("D"):
                rows(4, 6, "dSTc", NCH, "LNR")
                bc_evac(4, 6, "LNB", 2, "LNE")
                for c in range(NCH):
                    v.wait_ge(dXs[c % 3], 16 * (c // 3 + 1))
                    if c >= 3:
                        v.wait_ge(dX1Ss[c % 3], 16 * ((c - 3) // 3 + 1))
                        v.wait_ge(s["dST3c"], c - 2)
                        v.wait_ge(s["dSQ3"], c - 2)
                    ao = A2[:, c * T:(c + 1) * T]
                    xs = XS[:, (c % 3) * T:(c % 3 + 1) * T]
                    y1 = YT[:, (c % 3) * T:(c % 3 + 1) * T]
                    v.tensor_sub(y1, ao, BCA[:, :])
                    v.tensor_mul(y1, y1, BCB[:, :])
                    v.tensor_add(y1, y1, xs).then_inc(s["dX1"], 1)
                    v.tensor_copy(ao, y1)
                rows(0, 2, "dST3c", NCH, "LNR")
                bc_evac(0, 2, "LNB", 3, "LNE")

            if ph("D5"):
                for sl in range(3):
                    n_sl = len([c for c in range(NCH) if c % 3 == sl])
                    v.wait_ge(dX1Ss[sl], 16 * n_sl)
                for c in range(NCH):
                    tmp = YT[:, 0:T]
                    v.tensor_sub(tmp, A2[:, c * T:(c + 1) * T], BCA[:, :])
                    v.tensor_mul(A2[:, c * T:(c + 1) * T], tmp, BCB[:, :]).then_inc(s["eH"], 1)
                v.wait_ge(s["eGMc"], NCH)
                for hh in range(2):
                    vbar(v.tensor_copy(R[0:1, R_ABS + hh * 512:R_ABS + (hh + 1) * 512],
                                       P[4 + hh][0:1, :]))
                cp = None
                for hh in range(2):
                    cp = v.tensor_copy(R3[:, hh * 512:(hh + 1) * 512], P[6 + hh][0:3, :])
                cp.then_inc(s["eROW"], 1)
                vbar(v.tensor_reduce(SM[0:1, 0:1], R[0:1, R_ABS:R_ABS + VIS_N], AX, ALU.add))
                vbar(v.tensor_reduce(SM[0:1, 1:2], R[0:1, R_ABS + VIS_N:R_ABS + T], AX, ALU.add))
                vbar(v.tensor_add(SM[0:1, 2:3], SM[0:1, 0:1], SM[0:1, 1:2]))
                v.wait_ge(s["eGT"], 1)
                for tt in range(8):
                    vbar(v.tensor_copy(GT[:, tt * 4:tt * 4 + 3], Pb[4][:, tt * 4:tt * 4 + 3]))
                vbar(v.memset(CONTR[:, :], 0.0))
                vbar(v.tensor_mul(SM[0:1, 4:5], SM[0:1, 0:1], MMSK[0:1, 0:1]))
                vbar(v.tensor_mul(SM[0:1, 5:6], SM[0:1, 1:2], MMSK[0:1, 1:2]))
                vbar(v.tensor_mul(SM[0:1, 6:7], SM[0:1, 2:3], MMSK[0:1, 2:3]))
                vbar(v.tensor_add(SM[0:1, 5:6], SM[0:1, 5:6], SM[0:1, 6:7]))
                vbar(v.tensor_copy(CONTR[0:1, 0:1], SM[0:1, 4:5]))
                vbar(v.tensor_copy(CONTR[0:1, 1:2], SM[0:1, 5:6]))
                cp = v.tensor_mul(CONTR[0:1, 2:5], GT[0:1, 0:3], MMSK[0:1, 3:6])
                cp.then_inc(s["eMSK"], 1)
                v.wait_ge(s["eSL"], 16)
                vbar(v.tensor_scalar(SM[0:1, 2:3], STAT[0:1, 0:1], 1.0 / (VIS_N * H), None, ALU.mult))
                vbar(v.tensor_scalar(SM[0:1, 3:4], STAT[0:1, 1:2], 1.0 / (TEX_N * H), None, ALU.mult))
                vbar(v.tensor_scalar(SM[0:1, 2:3], SM[0:1, 2:3], 1e-8, None, ALU.add))
                vbar(v.reciprocal(SM[0:1, 4:5], SM[0:1, 2:3]))
                vbar(v.tensor_mul(SM[0:1, 5:6], SM[0:1, 3:4], SM[0:1, 4:5]))
                vbar(v.tensor_scalar(SM[0:1, 8:11], GLGB[0:1, 0:3], SM[0:1, 5:6], None, ALU.mult))
                vbar(v.tensor_add(SM[0:1, 8:11], SM[0:1, 8:11], GLGB[0:1, 3:6]))
                vbar(v.tensor_add(SM[0:1, 12:15], STAT[0:1, 2:5], SM[0:1, 8:11]))
                vbar(v.tensor_reduce(SM[0:1, 32:33], SM[0:1, 12:15], AX, ALU.max))
                vbar(v.tensor_reduce(SM[0:1, 33:34], SM[0:1, 12:15], AX, ALU.min))
                vbar(v.tensor_scalar(SM[0:1, 16:19], SM[0:1, 12:15], SM[0:1, 32:33], None, ALU.is_ge))
                vbar(v.tensor_scalar(SM[0:1, 24:27], SM[0:1, 12:15], SM[0:1, 32:33], None, ALU.is_lt))
                vbar(v.tensor_scalar(SM[0:1, 28:31], SM[0:1, 12:15], SM[0:1, 33:34], None, ALU.is_gt))
                cp = v.tensor_mul(SM[0:1, 24:27], SM[0:1, 24:27], SM[0:1, 28:31])
                cp.then_inc(s["eB3"], 1)
                v.wait_ge(s["eB3E"], 1)
                vbar(v.tensor_copy(SM3[:, 0:3], P[5][:, 0:3]))
                vbar(v.tensor_copy(SM3[:, 4:7], P[6][:, 0:3]))
                vbar(v.tensor_copy(SM3[:, 8:11], P[7][:, 0:3]))
                cp = None
                for tt in range(8):
                    vbar(v.tensor_add(GT2[:, tt * 4:tt * 4 + 3], GT[:, tt * 4:tt * 4 + 3],
                                      SM3[:, 0:3]))
                    cp = v.tensor_reduce(NM[:, tt:tt + 1], GT2[:, tt * 4:tt * 4 + 3], AX,
                                         ALU.max, negate=True)
                cp.then_inc(s["eGP"], 1)
                v.wait_ge(s["eGE"], 1)
                cp = None
                for tt in range(8):
                    ge = GE[:, tt * 4:tt * 4 + 3]
                    vbar(v.tensor_reduce(GSC[:, 0:1], ge, AX, ALU.add))
                    vbar(v.reciprocal(GSC[:, 1:2], GSC[:, 0:1]))
                    vbar(v.tensor_scalar(GT2[:, tt * 4:tt * 4 + 3], ge, GSC[:, 1:2], None, ALU.mult))
                    gs = GT2[:, tt * 4:tt * 4 + 3]
                    vbar(v.tensor_reduce(GSC[:, 2:3], gs, AX, ALU.max))
                    vbar(v.tensor_reduce(GSC[:, 3:4], gs, AX, ALU.min))
                    vbar(v.tensor_scalar(GSC[:, 4:5], GSC[:, 3:4], -1.0, 1.0 + 1e-8,
                                         ALU.mult, ALU.add))
                    vbar(v.reciprocal(GSC[:, 5:6], GSC[:, 4:5]))
                    vbar(v.tensor_mul(GSC[:, 6:7], GSC[:, 2:3], GSC[:, 5:6]))
                    vbar(v.tensor_sub(GSC[:, 7:8], GSC[:, 4:5], GSC[:, 2:3]))
                    vbar(v.tensor_scalar(GSC[:, 7:8], GSC[:, 7:8], -1e-8, None, ALU.add))
                    vbar(v.tensor_mul(GSC[:, 8:9], GSC[:, 7:8], GSC[:, 5:6]))
                    vbar(v.tensor_scalar(GSC[:, 9:12], SM3[:, 4:7], GSC[:, 6:7], None, ALU.mult))
                    vbar(v.tensor_scalar(GSC[:, 12:15], SM3[:, 8:11], GSC[:, 8:9], None, ALU.mult))
                    vbar(v.tensor_add(GSC[:, 9:12], GSC[:, 9:12], GSC[:, 12:15]))
                    for e in range(3):
                        cp = v.tensor_copy(CTB[:, e * 8 + tt:e * 8 + tt + 1],
                                           GSC[:, 9 + e:10 + e])
                cp.then_inc(s["eCT"], 1)
                v.wait_ge(s["eGTE"], 1)
                v.tensor_copy(CTT[:, :], Pb[5][0:24, 0:128]).then_inc(s["eCTE"], 1)
                for e in range(3):
                    v.wait_ge(s["eCB"], e + 1)
                    cp = None
                    for hh in range(2):
                        cp = v.tensor_copy(CEB[:, e * T + hh * 512:e * T + (hh + 1) * 512],
                                           P[6 + hh][:, :])
                    cp.then_inc(s["eCBE"], 1)

            if ph("E"):
                for e in range(3):
                    for fo in range(NCH):
                        i = e * NCH + fo
                        v.wait_ge(s["fM2"], i + 1)
                        if i >= 2:
                            v.wait_ge(fASs[i % 2], 16 * ((i - 2) // 2 + 1))
                        if e == 2 and fo >= 2:
                            v.wait_ge(s["fST4c"], fo - 1)
                        pb = 4 if i % 2 == 0 else 6
                        dst = ES[:, (fo % 2) * T:(fo % 2 + 1) * T]
                        b2c = B2T[:, e * NCH + fo:e * NCH + fo + 1]
                        cp = None
                        for hh in range(2):
                            cp = v.scalar_tensor_tensor(
                                dst[:, hh * 512:(hh + 1) * 512],
                                P[pb + hh][:, :],
                                b2c, CEB[:, e * T + hh * 512:e * T + (hh + 1) * 512],
                                ALU.add, ALU.mult)
                        if e > 0:
                            ip = (e - 1) * NCH + fo
                            v.wait_ge(fALs[fo % 2], 16 * (ip // 2 + 1))
                            for hh in range(2):
                                cp = v.tensor_add(
                                    dst[:, hh * 512:(hh + 1) * 512],
                                    dst[:, hh * 512:(hh + 1) * 512],
                                    ACS[:, (fo % 2) * T + hh * 512:(fo % 2) * T + (hh + 1) * 512])
                        cp.then_inc(s["fAEV"], 1)

            if ph("F"):
                rows(0, 2, "fST4c", NCH, "LNR")
                bc_evac(0, 2, "LNB", 4, "LNE")
                for c in range(NCH):
                    v.wait_ge(gAs[c % 2], 16 * (c // 2 + 1))
                    v.wait_ge(gX1s[c % 2], 16 * (c // 2 + 1))
                    acc = ACS[:, (c % 2) * T:(c % 2 + 1) * T]
                    x1c = YT[:, (c % 2) * T:(c % 2 + 1) * T]
                    tmp = XS[:, 0:T]
                    v.tensor_sub(tmp, acc, BCA[:, :])
                    v.tensor_mul(tmp, tmp, BCB[:, :])
                    v.tensor_add(x1c, tmp, x1c).then_inc(s["gY"], 1)

    return nc, dbg


def _restage(d):
    x = np.ascontiguousarray(np.asarray(d["L0_attn_x"], dtype=np.float32))
    qkvw = np.asarray(d["L0_attn_attn_qkvw"], dtype=np.float32)
    qkvb = np.asarray(d["L0_attn_attn_qkvb"], dtype=np.float32)
    ow = np.asarray(d["L0_attn_attn_ow"], dtype=np.float32)
    ob = np.asarray(d["L0_attn_attn_ob"], dtype=np.float32)
    gw = np.asarray(d["L0_ffn_gate_w"], dtype=np.float32)
    gb = np.asarray(d["L0_ffn_gate_b"], dtype=np.float32)
    w1 = np.asarray(d["L0_ffn_w1"], dtype=np.float32)
    b1 = np.asarray(d["L0_ffn_b1"], dtype=np.float32)
    w2 = np.asarray(d["L0_ffn_w2"], dtype=np.float32)
    b2 = np.asarray(d["L0_ffn_b2"], dtype=np.float32)

    shared = {
        "qkvw_t": np.ascontiguousarray(
            qkvw.reshape(32, 128, 96, 128).transpose(2, 1, 0, 3)).reshape(96, 128, H),
        "ow_t": np.ascontiguousarray(
            ow.reshape(32, 128, 32, 128).transpose(2, 1, 0, 3)).reshape(32, 128, H),
        "w1_t": np.ascontiguousarray(
            w1.reshape(3, 32, 128, 32, 128).transpose(0, 3, 2, 1, 4)).reshape(3, 32, 128, H),
        "w2_t": np.ascontiguousarray(
            w2.reshape(3, 32, 128, 32, 128).transpose(0, 3, 2, 1, 4)).reshape(3, 32, 128, H),
        "qkvb_t": np.ascontiguousarray(qkvb.reshape(96, 128).T),
        "ob_t": np.ascontiguousarray(ob.reshape(32, 128).T),
        "b1_t": np.ascontiguousarray(
            b1.reshape(3, 32, 128).transpose(2, 0, 1).reshape(128, 96)),
        "b2_t": np.ascontiguousarray(
            b2.reshape(3, 32, 128).transpose(2, 0, 1).reshape(128, 96)),
        "gww_t": np.ascontiguousarray(
            gw[0:H].reshape(32, 128, 3).transpose(1, 0, 2).reshape(128, 96)),
        "gl_gb": np.concatenate([gw[H], gb, np.zeros(2, np.float32)]
                                ).reshape(1, 8).astype(np.float32),
        "ident_in": np.eye(128, dtype=np.float32),
    }
    in_maps = []
    for c in range(8):
        b, hf = c // 2, c % 2
        m = dict(shared)
        m["x_in"] = np.ascontiguousarray(x[b, hf * T:(hf + 1) * T, :].T)
        ev = 1.0 if hf == 0 else 0.0
        m["mmask_in"] = np.array([[ev, ev, 1.0 - ev, ev, ev, ev, 0, 0]], np.float32)
        in_maps.append(m)
    return in_maps


def kernel(**inputs):
    nc, _ = build(upto="F", debug=False)
    in_maps = _restage(inputs)
    res = run_bass_kernel_spmd(nc, in_maps, core_ids=list(range(8)))
    out = np.empty((4, 2048, H), dtype=np.float32)
    for c in range(8):
        b, hf = c // 2, c % 2
        out[b, hf * T:(hf + 1) * T, :] = np.asarray(res.results[c]["y_out"]).T
    return out


# version 10 (final)
# speedup vs baseline: 1.0939x; 1.0939x over previous
import numpy as np
from contextlib import ExitStack

import concourse.bass as bass
import concourse.mybir as mybir
from concourse.bass_utils import run_bass_kernel_spmd

F32 = mybir.dt.float32
BF16 = mybir.dt.bfloat16
AF = mybir.ActivationFunctionType
ALU = mybir.AluOpType
AX = mybir.AxisListType.X

H = 4096
T = 1024
NCH = 32
HEADS = 32
HD = 128
SCALE = 1.0 / float(np.sqrt(HD))
CEXP = 30.0
VIS_N = 614
TEX_N = 2048 - VIS_N
EPS = 1e-5

PH_ALL = ["A", "B", "C", "D", "D5", "E", "F"]
D5_ILV = 16


def build(upto="F", debug=False):
    phases = PH_ALL[: PH_ALL.index(upto) + 1]

    def ph(p):
        return p in phases

    nc = bass.Bass("TRN2", target_bir_lowering=False, debug=False)

    x_in = nc.dram_tensor("x_in", [H, T], F32, kind="ExternalInput").ap()
    qkvw_t = nc.dram_tensor("qkvw_t", [96, 128, H], F32, kind="ExternalInput").ap()
    ow_t = nc.dram_tensor("ow_t", [NCH, 128, H], F32, kind="ExternalInput").ap()
    w1_t = nc.dram_tensor("w1_t", [3, NCH, 128, H], F32, kind="ExternalInput").ap()
    w2_t = nc.dram_tensor("w2_t", [3, NCH, 128, H], F32, kind="ExternalInput").ap()
    qkvb_t = nc.dram_tensor("qkvb_t", [128, 96], F32, kind="ExternalInput").ap()
    ob_t = nc.dram_tensor("ob_t", [128, NCH], F32, kind="ExternalInput").ap()
    b1_t = nc.dram_tensor("b1_t", [128, 3 * NCH], F32, kind="ExternalInput").ap()
    b2_t = nc.dram_tensor("b2_t", [128, 3 * NCH], F32, kind="ExternalInput").ap()
    gww_t = nc.dram_tensor("gww_t", [128, 3 * NCH], F32, kind="ExternalInput").ap()
    gl_gb = nc.dram_tensor("gl_gb", [1, 8], F32, kind="ExternalInput").ap()
    mmask_in = nc.dram_tensor("mmask_in", [1, 8], F32, kind="ExternalInput").ap()
    ident_in = nc.dram_tensor("ident_in", [128, 128], F32, kind="ExternalInput").ap()

    y_out = nc.dram_tensor("y_out", [H, T], F32, kind="ExternalOutput").ap()

    q_spill = nc.dram_tensor("q_spill", [H, T], BF16, kind="Internal").ap()
    kv_contrib = nc.dram_tensor("kv_contrib", [2 * H, T], BF16, kind="Internal").ap()
    kv_gath = nc.dram_tensor("kv_gath", [4 * H, T], BF16, kind="Internal").ap()
    x1_spill = nc.dram_tensor("x1_spill", [H, T], F32, kind="Internal").ap()
    acc_dram = nc.dram_tensor("acc_dram", [H, T], BF16, kind="Internal").ap()
    ct_dram = nc.dram_tensor("ct_dram", [3, 8 * 128], BF16, kind="Internal").ap()
    st_in = nc.dram_tensor("st_in", [1, 8], F32, kind="Internal").ap()
    st_out = nc.dram_tensor("st_out", [1, 8], F32, kind="Internal").ap()
    ctx_snap = nc.dram_tensor("ctx_snap", [128, NCH * T], BF16, kind="Internal").ap() \
        if debug else None

    dbg = {}
    if debug:
        for nm, shape in [
            ("dbg_q", [H, T]), ("dbg_kvg", [4 * H, T]), ("dbg_x1", [H, T]),
            ("dbg_acc", [H, T]), ("dbg_h", [128, NCH * T]), ("dbg_ct", [3, 8 * 128]),
            ("dbg_st", [1, 8]), ("dbg_cn", [1, 64]), ("dbg_ctx", [128, NCH * T]),
            ("dbg_xh", [128, NCH * T]),
        ]:
            dbg[nm] = nc.dram_tensor(nm, shape, F32, kind="ExternalOutput").ap()

    with ExitStack() as ctx:
        sb = lambda n, sh, d: ctx.enter_context(nc.sbuf_tensor(n, sh, d))
        psb = lambda n, sh, d: ctx.enter_context(nc.psum_tensor(n, sh, d))
        sem = lambda n: ctx.enter_context(nc.semaphore(n))

        A1 = sb("A1", [128, NCH * T], BF16)
        A2 = sb("A2", [128, NCH * T], BF16)
        WS = sb("WS", [128, 2 * H], BF16)
        XS = sb("XS", [128, 3 * T], F32)
        YT = sb("YT", [128, 3 * T], F32)
        ES = sb("ES", [128, 2 * T], BF16)
        ACS = sb("ACS", [128, 2 * T], BF16)
        SQ = sb("SQ", [128, 2 * T], BF16)
        BCA = sb("BCA", [128, T], F32)
        BCB = sb("BCB", [128, T], F32)
        R = sb("R", [1, 3 * T], F32)
        CROW = sb("CROW", [1, T], BF16)
        GT = sb("GT", [128, 32], F32)
        GT2 = sb("GT2", [128, 32], F32)
        GE = sb("GE", [128, 32], F32)
        NM = sb("NM", [128, 8], F32)
        GSC = sb("GSC", [128, 16], F32)
        SM3 = sb("SM3", [128, 16], F32)
        CTB = sb("CTB", [128, 24], BF16)
        CTT = sb("CTT", [24, 128], BF16)
        SM = sb("SM", [1, 64], F32)
        ONESB = sb("ONESB", [128, 1], BF16)
        CEXPB = sb("CEXPB", [128, 1], F32)
        ONESBF = sb("ONESBF", [128, 1], F32)
        ONESR = sb("ONESR", [1, 128], BF16)
        ONESRF = sb("ONESRF", [1, 128], F32)
        IDENB = sb("IDENB", [128, 128], BF16)
        QKVB = sb("QKVB", [128, 96], F32)
        QKVBS = sb("QKVBS", [128, NCH], F32)
        OBT = sb("OBT", [128, NCH], F32)
        B1T = sb("B1T", [128, 3 * NCH], F32)
        B2T = sb("B2T", [128, 3 * NCH], F32)
        GWWB = sb("GWWB", [128, 3 * NCH], BF16)
        GLGB = sb("GLGB", [1, 8], F32)
        MMSK = sb("MMSK", [1, 8], F32)
        CONTR = sb("CONTR", [1, 8], F32)
        STAT = sb("STAT", [1, 8], F32)

        R_M, R_T1, R_V = 0 * T, 1 * T, 2 * T
        R_RQ = R_V
        R_L, R_LI = R_M, R_V
        R_ABS = R_M

        LACB = YT[:, 0:512].bitcast(BF16)
        LCLB = YT[:, 512:1536]
        R3 = YT[:, 2048:2560].bitcast(BF16)[0:3, :]
        CEB = XS[:, 0:1536].bitcast(BF16)

        PT = A2[:, 0:16 * T]
        C0 = 16 * T
        KHB = A2[:, C0:C0 + 4096]
        VHB = A2[:, C0 + 4096:C0 + 8192]
        VTK = A2[:, C0 + 8192:C0 + 12288]
        QHB = A2[:, C0 + 12288:C0 + 14336]

        PA = [psb(f"PA{j}", [128, 1024], F32) for j in range(4)]
        P = [PA[i // 2][:, (i % 2) * 512:(i % 2 + 1) * 512] for i in range(8)]
        Pb = [p.bitcast(BF16) for p in P]

        names = [
            "pre", "preg", "preV",
            "aX0", "aX1", "aX2", "aX0g", "aX1g", "aX2g",
            "aSQ", "aSTc", "aXH", "aCP",
            "bW0", "bW1", "bMM", "bEV", "bST0", "bST1", "bCC",
            "cK0", "cK1", "cV0", "cV1", "cQ0", "cQ1",
            "cTR", "cTRV", "cS", "cE", "cL", "cLCV",
            "cLM", "cLE", "cLI", "cLB", "cLBE", "cCX", "cCV",
            "dW0", "dW1", "dMM", "dEV", "dSQ", "dSTc",
            "dX0", "dX1_", "dX2", "dX1", "dX1S0", "dX1S1", "dX1S2",
            "dSQ3", "dST3c", "LNR", "LNW", "LNB", "LNE",
            "eH", "eHA", "eGMc", "eROW", "eGT",
            "eMSK", "eCW", "eCC", "eSL",
            "eB3", "eB3E", "eGP", "eGE", "eCT", "eGTE", "eCTE", "eCTW", "eCRR", "eCB", "eCBE",
            "fW10", "fW11", "fM1", "fE1", "fM2",
            "fAL0", "fAL1", "fAEV", "fAS0", "fAS1",
            "fSQ4", "fST4c",
            "gA0", "gA1", "gX10", "gX11", "gX12", "gY", "gOUT", "gOUTg",
            "dbgS", "vSF",
        ]
        s = {n: sem(n) for n in names}
        aXs = [s["aX0"], s["aX1"], s["aX2"]]
        aXg = [s["aX0g"], s["aX1g"], s["aX2g"]]

        def ax_wait(eng, c):
            eng.wait_ge(aXs[c % 3] if c % 2 == 0 else aXg[c % 3],
                        16 * (c // 6 + 1))
        bWs = [s["bW0"], s["bW1"]]
        bSTs = [s["bST0"], s["bST1"]]
        cKs = [s["cK0"], s["cK1"]]
        cVs = [s["cV0"], s["cV1"]]
        cQs = [s["cQ0"], s["cQ1"]]
        dWs = [s["dW0"], s["dW1"]]
        dXs = [s["dX0"], s["dX1_"], s["dX2"]]
        dX1Ss = [s["dX1S0"], s["dX1S1"], s["dX1S2"]]
        fW1s = [s["fW10"], s["fW11"]]
        fALs = [s["fAL0"], s["fAL1"]]
        fASs = [s["fAS0"], s["fAS1"]]
        gAs = [s["gA0"], s["gA1"]]
        gX1s = [s["gX10"], s["gX11"], s["gX12"]]

        block = ctx.enter_context(nc.Block())

        fo_order = list(range(32, 96)) + list(range(32))

        esched = []
        if ph("E"):
            for e in range(3):
                for fo in range(NCH):
                    esched.append((w1_t[e, fo], "fM1", e * NCH + fo + 1))
                for fo in range(NCH):
                    esched.append((w2_t[e, fo], "fM2", e * NCH + fo + 1))

        fins = {
            "A": [(s["aXH"], NCH)],
            "B": [(bSTs[0], 16 * 48), (bSTs[1], 16 * 48)],
            "C": [(s["cCV"], HEADS)],
            "D": [(s["LNE"], 3)],
            "D5": [(s["eCBE"], 3)],
            "E": [(fASs[0], 16 * 48), (fASs[1], 16 * 48)],
            "F": [(s["gOUT"], 16 * (NCH // 2)), (s["gOUTg"], 16 * (NCH // 2))],
        }[upto]

        @block.gpsimd
        def _(g):
            g.dma_start(out=IDENB[:, :], in_=ident_in).then_inc(s["preg"], 16)
            g.dma_start(out=GWWB[:, :], in_=gww_t).then_inc(s["preg"], 16)
            if ph("A"):
                for c in range(1, NCH, 2):
                    if c >= 3:
                        g.wait_ge(s["aSQ"], c - 2)
                        g.wait_ge(s["aSTc"], c - 2)
                        g.wait_ge(s["aCP"], c - 2)
                    g.dma_start(out=XS[:, (c % 3) * T:(c % 3 + 1) * T],
                                in_=x_in[c * 128:(c + 1) * 128, :]).then_inc(aXg[c % 3], 16)
            if ph("B"):
                for i, fo in enumerate(fo_order):
                    if i == 64:
                        g.wait_ge(bSTs[0], 16 * 32)
                        g.wait_ge(bSTs[1], 16 * 32)
                        g.collective_compute(
                            "AllGather", ALU.bypass,
                            replica_groups=[[0, 1], [2, 3], [4, 5], [6, 7]],
                            ins=[kv_contrib], outs=[kv_gath],
                        ).then_inc(s["bCC"], 1)
                    if i >= 2:
                        g.wait_ge(s["bMM"], i - 1)
                    g.dma_start(out=WS[:, (i % 2) * H:(i % 2 + 1) * H],
                                in_=qkvw_t[fo]).then_inc(bWs[i % 2], 16)
            if ph("D"):
                for fo in range(NCH):
                    if fo < 2:
                        g.wait_ge(s["bMM"], 96)
                    else:
                        g.wait_ge(s["dMM"], fo - 1)
                    g.dma_start(out=WS[:, (fo % 2) * H:(fo % 2 + 1) * H],
                                in_=ow_t[fo]).then_inc(dWs[fo % 2], 16)
            if debug and ph("C"):
                g.wait_ge(s["cCV"], HEADS)
                g.dma_start(out=ctx_snap, in_=A1[:, :]).then_inc(s["dbgS"], 16)
            if ph("D5"):
                g.wait_ge(s["eCW"], 16)
                g.collective_compute(
                    "AllReduce", ALU.add,
                    replica_groups=[[0, 1], [2, 3], [4, 5], [6, 7]],
                    ins=[st_in], outs=[st_out],
                ).then_inc(s["eCC"], 1)
            if ph("E"):
                for j, (src, csem, ccnt) in enumerate(esched):
                    if j >= 2:
                        psem, pcnt = esched[j - 2][1], esched[j - 2][2]
                        g.wait_ge(s[psem], pcnt)
                    g.dma_start(out=WS[:, (j % 2) * H:(j % 2 + 1) * H],
                                in_=src).then_inc(fW1s[j % 2], 16)
            if ph("F"):
                g.wait_ge(s["eGT"], 1)
                g.wait_ge(s["eH"], NCH)
                for c in range(1, NCH, 2):
                    if c == 1:
                        g.wait_ge(fASs[0], 16 * 48)
                        g.wait_ge(fASs[1], 16 * 48)
                    if c >= 3:
                        g.wait_ge(s["gY"], c - 1)
                        g.dma_start(out=y_out[(c - 2) * 128:(c - 1) * 128, :],
                                    in_=YT[:, T:2 * T]).then_inc(s["gOUTg"], 16)
                    g.dma_start(out=YT[:, T:2 * T],
                                in_=x1_spill[c * 128:(c + 1) * 128, :]
                                ).then_inc(gX1s[1], 16)
                    g.dma_start(out=ACS[:, T:2 * T],
                                in_=acc_dram[c * 128:(c + 1) * 128, :]
                                ).then_inc(gAs[1], 16)
                g.wait_ge(s["gY"], NCH)
                g.dma_start(out=y_out[(NCH - 1) * 128:NCH * 128, :],
                            in_=YT[:, T:2 * T]).then_inc(s["gOUTg"], 16)
            if debug:
                for fs, fc in fins:
                    g.wait_ge(fs, fc)
                ndbg = 0
                if ph("A") and not ph("C"):
                    g.dma_start(out=dbg["dbg_xh"], in_=A2[:, 0:NCH * T]
                                ).then_inc(s["dbgS"], 16)
                    ndbg += 1
                if ph("B"):
                    g.wait_ge(s["bCC"], 1)
                    g.dma_start(out=dbg["dbg_q"], in_=q_spill).then_inc(s["dbgS"], 16)
                    g.dma_start(out=dbg["dbg_kvg"], in_=kv_gath).then_inc(s["dbgS"], 16)
                    ndbg += 2
                if ph("C"):
                    ndbg += 1
                    g.dma_start(out=dbg["dbg_ctx"], in_=ctx_snap).then_inc(s["dbgS"], 16)
                    ndbg += 1
                if ph("D"):
                    for sl in range(3):
                        n_sl = len([c for c in range(NCH) if c % 3 == sl])
                        g.wait_ge(dX1Ss[sl], 16 * n_sl)
                    g.dma_start(out=dbg["dbg_x1"], in_=x1_spill).then_inc(s["dbgS"], 16)
                    ndbg += 1
                if ph("D5") or ph("D"):
                    g.dma_start(out=dbg["dbg_h"], in_=A2[:, 0:NCH * T]).then_inc(s["dbgS"], 16)
                    ndbg += 1
                if ph("D5"):
                    g.dma_start(out=dbg["dbg_ct"], in_=ct_dram).then_inc(s["dbgS"], 16)
                    g.dma_start(out=dbg["dbg_st"], in_=st_out).then_inc(s["dbgS"], 16)
                    g.dma_start(out=dbg["dbg_cn"], in_=SM[:, :]).then_inc(s["dbgS"], 16)
                    ndbg += 3
                if ph("E"):
                    g.dma_start(out=dbg["dbg_acc"], in_=acc_dram).then_inc(s["dbgS"], 16)
                    ndbg += 1
                g.wait_ge(s["dbgS"], 16 * ndbg)

        @block.sync
        def _(sy):
            for ap_dst, ap_src in [
                (QKVB[:, :], qkvb_t), (OBT[:, :], ob_t), (B1T[:, :], b1_t),
                (B2T[:, :], b2_t), (GLGB[:, :], gl_gb), (MMSK[:, :], mmask_in),
            ]:
                sy.dma_start(out=ap_dst, in_=ap_src).then_inc(s["pre"], 16)
            if ph("A"):
                for c in range(0, NCH, 2):
                    if c >= 3:
                        sy.wait_ge(s["aSQ"], c - 2)
                        sy.wait_ge(s["aSTc"], c - 2)
                        sy.wait_ge(s["aCP"], c - 2)
                    sy.dma_start(out=XS[:, (c % 3) * T:(c % 3 + 1) * T],
                                 in_=x_in[c * 128:(c + 1) * 128, :]).then_inc(aXs[c % 3], 16)
            if ph("B"):
                for i, fo in enumerate(fo_order):
                    sy.wait_ge(s["bEV"], i + 1)
                    tile = ES[:, (i % 2) * T:(i % 2 + 1) * T]
                    if fo < 32:
                        sy.dma_start(out=q_spill[fo * 128:(fo + 1) * 128, :],
                                     in_=tile).then_inc(bSTs[i % 2], 16)
                    else:
                        sy.dma_start(out=kv_contrib[(fo - 32) * 128:(fo - 31) * 128, :],
                                     in_=tile).then_inc(bSTs[i % 2], 16)
            if ph("C"):
                for h in range(HEADS):
                    hb = h % 2
                    if h == 0:
                        sy.wait_ge(s["bCC"], 1)
                        sy.wait_ge(s["bMM"], 96)
                    if h >= 2:
                        sy.wait_ge(s["cS"], 16 * (h - 1))
                        sy.wait_ge(s["cTR"], h - 1)
                    for r in range(2):
                        sy.dma_start(
                            out=KHB[:, hb * 2048 + r * T:hb * 2048 + (r + 1) * T],
                            in_=kv_gath[r * 2 * H + h * 128:r * 2 * H + (h + 1) * 128, :]
                        ).then_inc(cKs[hb], 16)
                        sy.dma_start(
                            out=VHB[:, hb * 2048 + r * T:hb * 2048 + (r + 1) * T],
                            in_=kv_gath[r * 2 * H + H + h * 128:r * 2 * H + H + (h + 1) * 128, :]
                        ).then_inc(cVs[hb], 16)
                    qi = 64 + h
                    sy.wait_ge(bSTs[qi % 2], 16 * (qi // 2 + 1))
                    sy.dma_start(out=QHB[:, hb * T:(hb + 1) * T],
                                 in_=q_spill[h * 128:(h + 1) * 128, :]).then_inc(cQs[hb], 16)
            if ph("D"):
                for c in range(NCH):
                    if c == 0:
                        sy.wait_ge(s["aSTc"], NCH)
                        sy.wait_ge(s["aCP"], NCH)
                        sy.wait_ge(s["aSQ"], NCH)
                    sy.dma_start(out=XS[:, (c % 3) * T:(c % 3 + 1) * T],
                                 in_=x_in[c * 128:(c + 1) * 128, :]).then_inc(dXs[c % 3], 16)
                    if c >= 1:
                        cs = c - 1
                        sy.wait_ge(s["dX1"], cs + 1)
                        sy.dma_start(out=x1_spill[cs * 128:(cs + 1) * 128, :],
                                     in_=YT[:, (cs % 3) * T:(cs % 3 + 1) * T]
                                     ).then_inc(dX1Ss[cs % 3], 16)
                sy.wait_ge(s["dX1"], NCH)
                sy.dma_start(out=x1_spill[(NCH - 1) * 128:NCH * 128, :],
                             in_=YT[:, ((NCH - 1) % 3) * T:((NCH - 1) % 3 + 1) * T]
                             ).then_inc(dX1Ss[(NCH - 1) % 3], 16)
            if ph("D5"):
                sy.wait_ge(s["eMSK"], 1)
                sy.dma_start(out=st_in, in_=CONTR[:, :]).then_inc(s["eCW"], 16)
                sy.wait_ge(s["eCC"], 1)
                sy.dma_start(out=STAT[:, :], in_=st_out).then_inc(s["eSL"], 16)
                sy.wait_ge(s["eCTE"], 1)
                sy.dma_start(out=ct_dram.rearrange("e (t p) -> (e t) p", p=128),
                             in_=CTT[:, :]).then_inc(s["eCTW"], 16)
                sy.wait_ge(s["eCTW"], 16)
                for e in range(3):
                    if e >= 1:
                        sy.wait_ge(s["eCB"], e)
                    sy.dma_start(out=CROW[:, :],
                                 in_=ct_dram[e:e + 1, :]).then_inc(s["eCRR"], 16)
            if ph("E"):
                for e in range(3):
                    for fo in range(NCH):
                        if e > 0:
                            ip = (e - 1) * NCH + fo
                            sy.wait_ge(fASs[ip % 2], 16 * (ip // 2 + 1))
                            if fo >= 2:
                                sy.wait_ge(s["fAEV"], e * NCH + fo - 1)
                            sy.dma_start(out=ACS[:, (fo % 2) * T:(fo % 2 + 1) * T],
                                         in_=acc_dram[fo * 128:(fo + 1) * 128, :]
                                         ).then_inc(fALs[fo % 2], 16)
                        if fo >= 1:
                            fs = fo - 1
                            sy.wait_ge(s["fAEV"], e * NCH + fs + 1)
                            sy.dma_start(out=acc_dram[fs * 128:(fs + 1) * 128, :],
                                         in_=ES[:, (fs % 2) * T:(fs % 2 + 1) * T]
                                         ).then_inc(fASs[fs % 2], 16)
                    fs = NCH - 1
                    sy.wait_ge(s["fAEV"], e * NCH + fs + 1)
                    sy.dma_start(out=acc_dram[fs * 128:(fs + 1) * 128, :],
                                 in_=ES[:, (fs % 2) * T:(fs % 2 + 1) * T]
                                 ).then_inc(fASs[fs % 2], 16)
            if ph("F"):
                for c in range(0, NCH, 2):
                    if c == 0:
                        sy.wait_ge(fASs[0], 16 * 48)
                        sy.wait_ge(fASs[1], 16 * 48)
                        sy.wait_ge(s["eGT"], 1)
                        sy.wait_ge(s["eH"], NCH)
                    if c >= 2:
                        sy.wait_ge(s["gY"], c - 1)
                        sy.dma_start(out=y_out[(c - 2) * 128:(c - 1) * 128, :],
                                     in_=YT[:, 0:T]).then_inc(s["gOUT"], 16)
                    sy.dma_start(out=YT[:, 0:T],
                                 in_=x1_spill[c * 128:(c + 1) * 128, :]
                                 ).then_inc(gX1s[0], 16)
                    sy.dma_start(out=ACS[:, 0:T],
                                 in_=acc_dram[c * 128:(c + 1) * 128, :]
                                 ).then_inc(gAs[0], 16)
                sy.wait_ge(s["gY"], NCH - 1)
                sy.dma_start(out=y_out[(NCH - 2) * 128:(NCH - 1) * 128, :],
                             in_=YT[:, 0:T]).then_inc(s["gOUT"], 16)
            for fs, fc in fins:
                sy.wait_ge(fs, fc)

        @block.tensor
        def _(t):
            t.wait_ge(s["preg"], 32)
            t.wait_ge(s["preV"], 1)

            def stats4(c, src_f32, src_bf, sq, b1, b2, w1s, w1c, w2s, w2c, tagc):
                t.wait_ge(s[w1s], w1c)
                t.wait_ge(s[w2s], w2c)
                for hh in range(2):
                    if src_f32 is not None:
                        t.matmul(P[b1 + hh][0:1, :], ONESBF[:, :],
                                 src_f32[:, hh * 512:(hh + 1) * 512],
                                 start=(c == 0), stop=(c == NCH - 1))
                    else:
                        t.matmul(P[b1 + hh][0:1, :], ONESB[:, :],
                                 src_bf[:, hh * 512:(hh + 1) * 512],
                                 start=(c == 0), stop=(c == NCH - 1))
                mm = None
                for hh in range(2):
                    mm = t.matmul(P[b2 + hh][0:1, :], ONESB[:, :],
                                  sq[:, hh * 512:(hh + 1) * 512],
                                  start=(c == 0), stop=(c == NCH - 1))
                mm.then_inc(s[tagc], 1)

            def bcast(b1, b2, wsem, wcnt, tag):
                t.wait_ge(s[wsem], wcnt)
                for hh in range(2):
                    t.matmul(P[b1 + hh][:, :], ONESRF[:, :],
                             R[0:1, R_M + hh * 512:R_M + (hh + 1) * 512],
                             start=True, stop=True)
                mm = None
                for hh in range(2):
                    mm = t.matmul(P[b2 + hh][:, :], ONESRF[:, :],
                                  R[0:1, R_RQ + hh * 512:R_RQ + (hh + 1) * 512],
                                  start=True, stop=True)
                mm.then_inc(s[tag], 1)

            if ph("A"):
                for c in range(NCH):
                    stats4(c, XS[:, (c % 3) * T:(c % 3 + 1) * T], None,
                           SQ[:, (c % 2) * T:(c % 2 + 1) * T],
                           0, 2,
                           "aX%d" % (c % 3) if c % 2 == 0 else "aX%dg" % (c % 3),
                           16 * (c // 6 + 1), "aSQ", c + 1, "aSTc")
                bcast(4, 6, "LNW", 1, "LNB")

            if ph("B"):
                for i, fo in enumerate(fo_order):
                    t.wait_ge(bWs[i % 2], 16 * (i // 2 + 1))
                    if i == 0:
                        t.wait_ge(s["aXH"], NCH)
                    if i >= 2:
                        t.wait_ge(s["bEV"], i - 1)
                    pb = 0 if i % 2 == 0 else 2
                    w = WS[:, (i % 2) * H:(i % 2 + 1) * H]
                    mm = None
                    for fin in range(NCH):
                        lhs = w[:, fin * 128:(fin + 1) * 128]
                        for hh in range(2):
                            mm = t.matmul(P[pb + hh][:, :], lhs,
                                          A2[:, fin * T + hh * 512:fin * T + (hh + 1) * 512],
                                          start=(fin == 0), stop=(fin == NCH - 1))
                    mm.then_inc(s["bMM"], 1)

            if ph("C"):
                for h in range(HEADS):
                    hb = h % 2
                    t.wait_ge(cVs[hb], 32 * (h // 2 + 1))
                    if h >= 1:
                        t.wait_ge(s["cLBE"], h)
                    if h >= 2:
                        t.wait_ge(s["cCX"], h - 1)
                    mm = None
                    for k in range(16):
                        bank = 6 + (k // 8)
                        off = (k % 8) * 128
                        mm = t.transpose(Pb[bank][:, off:off + 128],
                                         VHB[:, hb * 2048 + k * 128:hb * 2048 + (k + 1) * 128],
                                         IDENB[:, :])
                    mm.then_inc(s["cTR"], 1)
                    t.wait_ge(cKs[hb], 32 * (h // 2 + 1))
                    t.wait_ge(cQs[hb], 16 * (h // 2 + 1))
                    for k in range(16):
                        pib = 0 if k % 2 == 0 else 2
                        t.wait_ge(s["cE"], max(0, 16 * h + k - 1))
                        mm = None
                        for hh in range(2):
                            mm = t.matmul(P[pib + hh][:, :],
                                          KHB[:, hb * 2048 + k * 128:hb * 2048 + (k + 1) * 128],
                                          QHB[:, hb * T + hh * 512:hb * T + (hh + 1) * 512],
                                          start=True, stop=True)
                        mm.then_inc(s["cS"], 1)
                        if k >= 1:
                            kc = k - 1
                            t.wait_ge(s["cE"], 16 * h + kc + 1)
                            if kc == 0:
                                t.wait_ge(s["cCV"], h)
                                t.wait_ge(s["cTRV"], h + 1)
                            for hh in range(2):
                                t.matmul(P[4 + hh][:, :],
                                         VTK[:, hb * 2048 + kc * 128:hb * 2048 + (kc + 1) * 128],
                                         PT[:, kc * T + hh * 512:kc * T + (hh + 1) * 512],
                                         start=(kc == 0), stop=False)
                    t.wait_ge(s["cE"], 16 * h + 16)
                    mm = None
                    for hh in range(2):
                        mm = t.matmul(P[4 + hh][:, :],
                                      VTK[:, hb * 2048 + 15 * 128:hb * 2048 + 16 * 128],
                                      PT[:, 15 * T + hh * 512:15 * T + (hh + 1) * 512],
                                      start=False, stop=True)
                    mm.then_inc(s["cCX"], 1)
                    t.wait_ge(s["cLCV"], h + 1)
                    t.wait_ge(s["cTRV"], h + 1)
                    mm = None
                    for hh in range(2):
                        mm = t.matmul(P[6 + hh][0:1, :], ONESB[:, :],
                                      LACB[:, hh * 512:(hh + 1) * 512], start=True, stop=True)
                    mm.then_inc(s["cLM"], 1)
                    t.wait_ge(s["cLI"], h + 1)
                    mm = None
                    for hh in range(2):
                        mm = t.matmul(P[6 + hh][:, :], ONESRF[:, :],
                                      R[0:1, R_LI + hh * 512:R_LI + (hh + 1) * 512],
                                      start=True, stop=True)
                    mm.then_inc(s["cLB"], 1)

            if ph("D"):
                for fo in range(NCH):
                    t.wait_ge(dWs[fo % 2], 16 * (fo // 2 + 1))
                    if fo == 0:
                        t.wait_ge(s["cCV"], HEADS)
                    if fo >= 2:
                        t.wait_ge(s["dEV"], fo - 1)
                    pb = 0 if fo % 2 == 0 else 2
                    w = WS[:, (fo % 2) * H:(fo % 2 + 1) * H]
                    mm = None
                    for fin in range(NCH):
                        lhs = w[:, fin * 128:(fin + 1) * 128]
                        for hh in range(2):
                            mm = t.matmul(P[pb + hh][:, :], lhs,
                                          A1[:, fin * T + hh * 512:fin * T + (hh + 1) * 512],
                                          start=(fin == 0), stop=(fin == NCH - 1))
                    mm.then_inc(s["dMM"], 1)
                    if fo >= 2:
                        c = fo - 2
                        stats4(c, None, A2[:, c * T:(c + 1) * T],
                               SQ[:, (c % 2) * T:(c % 2 + 1) * T],
                               4, 6, "dEV", c + 1, "dSQ", c + 1, "dSTc")
                for c in [NCH - 2, NCH - 1]:
                    stats4(c, None, A2[:, c * T:(c + 1) * T],
                           SQ[:, (c % 2) * T:(c % 2 + 1) * T],
                           4, 6, "dEV", c + 1, "dSQ", c + 1, "dSTc")
                bcast(4, 6, "LNW", 2, "LNB")
                for c in range(NCH):
                    stats4(c, YT[:, (c % 3) * T:(c % 3 + 1) * T], None,
                           SQ[:, (c % 2) * T:(c % 2 + 1) * T],
                           0, 2, "dX1", c + 1, "dSQ3", c + 1, "dST3c")
                bcast(0, 2, "LNW", 3, "LNB")

            if ph("D5"):
                for c in range(NCH):
                    t.wait_ge(s["eHA"], c + 1)
                    for hh in range(2):
                        t.matmul(P[4 + hh][0:1, :], ONESB[:, :],
                                 ES[:, (c % 2) * T + hh * 512:(c % 2) * T + (hh + 1) * 512],
                                 start=(c == 0), stop=(c == NCH - 1))
                    mm = None
                    for hh in range(2):
                        mm = t.matmul(P[6 + hh][0:3, :], GWWB[:, c * 3:(c + 1) * 3],
                                      A2[:, c * T + hh * 512:c * T + (hh + 1) * 512],
                                      start=(c == 0), stop=(c == NCH - 1))
                    mm.then_inc(s["eGMc"], 1)
                t.wait_ge(s["eROW"], 1)
                mm = None
                for tt in range(8):
                    mm = t.transpose(Pb[4][:, tt * 4:tt * 4 + 3],
                                     R3[:, tt * 128:(tt + 1) * 128], IDENB[0:3, 0:3])
                mm.then_inc(s["eGT"], 1)

            def d5_tail(t):
                t.wait_ge(s["eB3"], 1)
                t.matmul(P[5][:, 0:3], ONESRF[:, :], SM[0:1, 8:11], start=True, stop=True)
                t.matmul(P[6][:, 0:3], ONESRF[:, :], SM[0:1, 16:19], start=True, stop=True)
                mm = t.matmul(P[7][:, 0:3], ONESRF[:, :], SM[0:1, 24:27], start=True, stop=True)
                mm.then_inc(s["eB3E"], 1)
                t.wait_ge(s["eCT"], 1)
                mm = t.transpose(Pb[5][0:24, 0:128], CTB[:, :], IDENB[:, :])
                mm.then_inc(s["eGTE"], 1)
                for e in range(3):
                    t.wait_ge(s["eCRR"], 16 * (e + 1))
                    if e >= 1:
                        t.wait_ge(s["eCBE"], e)
                    mm = None
                    for hh in range(2):
                        mm = t.matmul(P[6 + hh][:, :], ONESR[:, :],
                                      CROW[0:1, hh * 512:(hh + 1) * 512],
                                      start=True, stop=True)
                    mm.then_inc(s["eCB"], 1)

            if ph("D5") and (not ph("E") or D5_ILV == 0):
                d5_tail(t)

            if ph("E"):
                for e in range(3):
                    for fo in range(NCH):
                        i = e * NCH + fo
                        jj = e * 2 * NCH + fo
                        if D5_ILV and i == D5_ILV:
                            d5_tail(t)
                        t.wait_ge(fW1s[jj % 2], 16 * (jj // 2 + 1))
                        if i == 0:
                            t.wait_ge(s["eH"], NCH)
                        if i >= 2:
                            t.wait_ge(s["fE1"], i - 1)
                        if e > 0 and fo == 0:
                            t.wait_ge(s["fM2"], e * NCH)
                        pb = 0 if i % 2 == 0 else 2
                        w = WS[:, ((e * 2 * NCH + fo) % 2) * H:
                               ((e * 2 * NCH + fo) % 2 + 1) * H]
                        mm = None
                        for fin in range(NCH):
                            lhs = w[:, fin * 128:(fin + 1) * 128]
                            for hh in range(2):
                                mm = t.matmul(P[pb + hh][:, :], lhs,
                                              A2[:, fin * T + hh * 512:fin * T + (hh + 1) * 512],
                                              start=(fin == 0), stop=(fin == NCH - 1))
                        mm.then_inc(s["fM1"], 1)
                    for fo in range(NCH):
                        i = e * NCH + fo
                        jj = e * 2 * NCH + NCH + fo
                        t.wait_ge(fW1s[jj % 2], 16 * (jj // 2 + 1))
                        if i == 0:
                            t.wait_ge(s["eCBE"], 3)
                        if fo == 0:
                            t.wait_ge(s["fE1"], e * NCH + NCH)
                        if i >= 2:
                            t.wait_ge(s["fAEV"], i - 1)
                        pb = 4 if i % 2 == 0 else 6
                        w = WS[:, ((e * 2 * NCH + NCH + fo) % 2) * H:
                               ((e * 2 * NCH + NCH + fo) % 2 + 1) * H]
                        mm = None
                        for hh in range(2):
                            for fin in range(NCH):
                                lhs = w[:, fin * 128:(fin + 1) * 128]
                                mm = t.matmul(P[pb + hh][:, :], lhs,
                                              A1[:, fin * T + hh * 512:fin * T + (hh + 1) * 512],
                                              start=(fin == 0), stop=(fin == NCH - 1))
                        mm.then_inc(s["fM2"], 1)
                        if e == 2 and fo >= 2:
                            c = fo - 2
                            stats4(c, None, ES[:, (c % 2) * T:(c % 2 + 1) * T],
                                   SQ[:, (c % 2) * T:(c % 2 + 1) * T],
                                   0, 2, "fAEV", 2 * NCH + c + 1, "fSQ4", c + 1, "fST4c")
                for c in [NCH - 2, NCH - 1]:
                    stats4(c, None, ES[:, (c % 2) * T:(c % 2 + 1) * T],
                           SQ[:, (c % 2) * T:(c % 2 + 1) * T],
                           0, 2, "fAEV", 2 * NCH + c + 1, "fSQ4", c + 1, "fST4c")
                bcast(0, 2, "LNW", 4, "LNB")

        @block.scalar
        def _(sc):
            sc.wait_ge(s["pre"], 96)
            sc.wait_ge(s["preg"], 32)

            def rsqrt_rows(wsem, wcnt, tag):
                sc.wait_ge(s[wsem], wcnt)
                sc.activation(R[0:1, R_T1:R_T1 + T], R[0:1, R_V:R_V + T], AF.Ln)
                sc.activation(R[0:1, R_RQ:R_RQ + T], R[0:1, R_T1:R_T1 + T], AF.Exp,
                              scale=-0.5).then_inc(s[tag], 1)

            if ph("A"):
                for c in range(NCH):
                    ax_wait(sc, c)
                    if c >= 2:
                        sc.wait_ge(s["aSTc"], c - 1)
                    sc.activation(SQ[:, (c % 2) * T:(c % 2 + 1) * T],
                                  XS[:, (c % 3) * T:(c % 3 + 1) * T],
                                  AF.Square).then_inc(s["aSQ"], 1)
                rsqrt_rows("LNR", 1, "LNW")

            if ph("B"):
                sc.wait_ge(s["preV"], 1)
                for i, fo in enumerate(fo_order):
                    sc.wait_ge(s["bMM"], i + 1)
                    if i >= 2:
                        sc.wait_ge(bSTs[i % 2], 16 * ((i - 2) // 2 + 1))
                    pb = 0 if i % 2 == 0 else 2
                    tile = ES[:, (i % 2) * T:(i % 2 + 1) * T]
                    if fo < 32:
                        bias, scl = QKVBS[:, fo:fo + 1], SCALE
                    else:
                        bias, scl = QKVB[:, fo:fo + 1], 1.0
                    a = None
                    for hh in range(2):
                        a = sc.activation(tile[:, hh * 512:(hh + 1) * 512], P[pb + hh][:, :],
                                          AF.Identity, bias=bias, scale=scl)
                    a.then_inc(s["bEV"], 1)

            if ph("C"):
                for h in range(HEADS):
                    for k in range(16):
                        sc.wait_ge(s["cS"], 16 * h + k + 1)
                        a = sc.activation(PT[:, k * T:(k + 1) * T], PA[k % 2][:, :],
                                          AF.Exp, bias=CEXPB[:, :])
                        a.then_inc(s["cE"], 1)
                    if h >= 1:
                        sc.wait_ge(s["cLM"], h)
                    sc.wait_ge(s["cL"], h + 1)
                    sc.activation(LACB[:, :], LCLB[:, :], AF.Copy).then_inc(s["cLCV"], 1)
                    sc.wait_ge(s["cLM"], h + 1)
                    for hh in range(2):
                        sc.activation(R[0:1, R_L + hh * 512:R_L + (hh + 1) * 512],
                                      P[6 + hh][0:1, :], AF.Copy)
                    sc.activation(R[0:1, R_T1:R_T1 + T], R[0:1, R_L:R_L + T], AF.Ln)
                    sc.activation(R[0:1, R_LI:R_LI + T], R[0:1, R_T1:R_T1 + T], AF.Exp,
                                  scale=-1.0).then_inc(s["cLI"], 1)

            if ph("D"):
                for fo in range(NCH):
                    sc.wait_ge(s["dMM"], fo + 1)
                    pb = 0 if fo % 2 == 0 else 2
                    dst = A2[:, fo * T:(fo + 1) * T]
                    a = None
                    for hh in range(2):
                        a = sc.activation(dst[:, hh * 512:(hh + 1) * 512], P[pb + hh][:, :],
                                          AF.Identity, bias=OBT[:, fo:fo + 1])
                    a.then_inc(s["dEV"], 1)
                    if fo >= 2:
                        sc.wait_ge(s["dSTc"], fo - 1)
                    sc.activation(SQ[:, (fo % 2) * T:(fo % 2 + 1) * T], dst,
                                  AF.Square).then_inc(s["dSQ"], 1)
                rsqrt_rows("LNR", 2, "LNW")
                for c in range(NCH):
                    sc.wait_ge(s["dX1"], c + 1)
                    if c >= 2:
                        sc.wait_ge(s["dST3c"], c - 1)
                    sc.activation(SQ[:, (c % 2) * T:(c % 2 + 1) * T],
                                  YT[:, (c % 3) * T:(c % 3 + 1) * T],
                                  AF.Square).then_inc(s["dSQ3"], 1)
                rsqrt_rows("LNR", 3, "LNW")

            if ph("D5"):
                for c in range(NCH):
                    sc.wait_ge(s["eH"], c + 1)
                    if c < 2:
                        sc.wait_ge(s["dST3c"], NCH)
                    else:
                        sc.wait_ge(s["eGMc"], c - 1)
                    sc.activation(ES[:, (c % 2) * T:(c % 2 + 1) * T],
                                  A2[:, c * T:(c + 1) * T], AF.Abs).then_inc(s["eHA"], 1)

            def d5_ge(sc):
                sc.wait_ge(s["eGP"], 1)
                a = None
                for tt in range(8):
                    a = sc.activation(GE[:, tt * 4:tt * 4 + 3], GT2[:, tt * 4:tt * 4 + 3],
                                      AF.Exp, bias=NM[:, tt:tt + 1])
                a.then_inc(s["eGE"], 1)

            if ph("D5") and (not ph("E") or D5_ILV == 0):
                d5_ge(sc)

            if ph("E"):
                act_f = [AF.Tanh, AF.Gelu, AF.Silu]
                for e in range(3):
                    for fo in range(NCH):
                        i = e * NCH + fo
                        if D5_ILV and i == D5_ILV:
                            d5_ge(sc)
                        sc.wait_ge(s["fM1"], i + 1)
                        pb = 0 if i % 2 == 0 else 2
                        dst = A1[:, fo * T:(fo + 1) * T]
                        a = None
                        for hh in range(2):
                            a = sc.activation(dst[:, hh * 512:(hh + 1) * 512], P[pb + hh][:, :],
                                              act_f[e],
                                              bias=B1T[:, e * NCH + fo:e * NCH + fo + 1])
                        a.then_inc(s["fE1"], 1)
                for c in range(NCH):
                    sc.wait_ge(s["fAEV"], 2 * NCH + c + 1)
                    if c >= 2:
                        sc.wait_ge(s["fST4c"], c - 1)
                    sc.activation(SQ[:, (c % 2) * T:(c % 2 + 1) * T],
                                  ES[:, (c % 2) * T:(c % 2 + 1) * T],
                                  AF.Square).then_inc(s["fSQ4"], 1)

            if ph("F"):
                rsqrt_rows("LNR", 4, "LNW")

        @block.vector
        def _(v):
            v.memset(ONESB[:, :], 1.0)
            v.memset(CEXPB[:, :], -CEXP)
            v.memset(ONESBF[:, :], 1.0)
            v.memset(ONESR[:, :], 1.0)
            v.memset(ONESRF[:, :], 1.0)
            v.wait_ge(s["pre"], 96)
            v.tensor_scalar(QKVBS[:, :], QKVB[:, 0:NCH], SCALE, None,
                            ALU.mult).then_inc(s["preV"], 1)

            def rows(b1, b2, wsem, wcnt, tag):
                v.wait_ge(s[wsem], wcnt)
                for hh in range(2):
                    v.tensor_scalar(R[0:1, R_M + hh * 512:R_M + (hh + 1) * 512],
                                    P[b1 + hh][0:1, :], 1.0 / H, None, ALU.mult)
                for hh in range(2):
                    v.tensor_scalar(R[0:1, R_T1 + hh * 512:R_T1 + (hh + 1) * 512],
                                    P[b2 + hh][0:1, :], 1.0 / H, None, ALU.mult)
                v.tensor_mul(R[0:1, R_V:R_V + T], R[0:1, R_M:R_M + T], R[0:1, R_M:R_M + T])
                v.tensor_sub(R[0:1, R_V:R_V + T], R[0:1, R_T1:R_T1 + T], R[0:1, R_V:R_V + T])
                v.tensor_scalar(R[0:1, R_V:R_V + T], R[0:1, R_V:R_V + T], EPS, None,
                                ALU.add).then_inc(s[tag], 1)

            vsf = [0]

            def vbar(ins):
                vsf[0] += 1
                ins.then_inc(s["vSF"], 1)
                v.wait_ge(s["vSF"], vsf[0])

            def bc_evac(b1, b2, wsem, wcnt, tag):
                v.wait_ge(s[wsem], wcnt)
                for hh in range(2):
                    v.tensor_copy(BCA[:, hh * 512:(hh + 1) * 512], P[b1 + hh][:, :])
                cp = None
                for hh in range(2):
                    cp = v.tensor_copy(BCB[:, hh * 512:(hh + 1) * 512], P[b2 + hh][:, :])
                cp.then_inc(s[tag], 1)

            if ph("A"):
                for c in range(NCH):
                    ax_wait(v, c)
                    v.tensor_copy(A1[:, c * T:(c + 1) * T],
                                  XS[:, (c % 3) * T:(c % 3 + 1) * T]).then_inc(s["aCP"], 1)
                rows(0, 2, "aSTc", NCH, "LNR")
                bc_evac(4, 6, "LNB", 1, "LNE")
                for c in range(NCH):
                    tmp = YT[:, (c % 3) * T:(c % 3 + 1) * T]
                    v.tensor_sub(tmp, A1[:, c * T:(c + 1) * T], BCA[:, :])
                    v.tensor_mul(A2[:, c * T:(c + 1) * T], tmp, BCB[:, :]).then_inc(s["aXH"], 1)

            if ph("C"):
                for h in range(HEADS):
                    hb = h % 2
                    v.wait_ge(s["cTR"], h + 1)
                    cp = None
                    for k in range(16):
                        bank = 6 + (k // 8)
                        off = (k % 8) * 128
                        cp = v.tensor_copy(VTK[:, hb * 2048 + k * 128:hb * 2048 + (k + 1) * 128],
                                           Pb[bank][:, off:off + 128])
                    cp.then_inc(s["cTRV"], 1)
                    if h >= 1:
                        v.wait_ge(s["cLCV"], h)
                    cp = None
                    for k in range(16):
                        v.wait_ge(s["cE"], 16 * h + k + 1)
                        if k == 0:
                            cp = v.tensor_copy(LCLB[:, :], PT[:, 0:T])
                        else:
                            cp = v.tensor_add(LCLB[:, :], LCLB[:, :], PT[:, k * T:(k + 1) * T])
                    cp.then_inc(s["cL"], 1)
                    v.wait_ge(s["cLB"], h + 1)
                    cp = None
                    for hh in range(2):
                        cp = v.tensor_copy(LCLB[:, hh * 512:(hh + 1) * 512], P[6 + hh][:, :])
                    cp.then_inc(s["cLBE"], 1)
                    v.wait_ge(s["cCX"], h + 1)
                    cp = None
                    for hh in range(2):
                        cp = v.tensor_tensor(A1[:, h * T + hh * 512:h * T + (hh + 1) * 512],
                                             P[4 + hh][:, :], LCLB[:, hh * 512:(hh + 1) * 512],
                                             ALU.mult)
                    cp.then_inc(s["cCV"], 1)

            if ph("D"):
                rows(4, 6, "dSTc", NCH, "LNR")
                bc_evac(4, 6, "LNB", 2, "LNE")
                for c in range(NCH):
                    v.wait_ge(dXs[c % 3], 16 * (c // 3 + 1))
                    if c >= 3:
                        v.wait_ge(dX1Ss[c % 3], 16 * ((c - 3) // 3 + 1))
                        v.wait_ge(s["dST3c"], c - 2)
                        v.wait_ge(s["dSQ3"], c - 2)
                    ao = A2[:, c * T:(c + 1) * T]
                    xs = XS[:, (c % 3) * T:(c % 3 + 1) * T]
                    y1 = YT[:, (c % 3) * T:(c % 3 + 1) * T]
                    v.tensor_sub(y1, ao, BCA[:, :])
                    v.tensor_mul(y1, y1, BCB[:, :])
                    v.tensor_add(y1, y1, xs).then_inc(s["dX1"], 1)
                    v.tensor_copy(ao, y1)
                rows(0, 2, "dST3c", NCH, "LNR")
                bc_evac(0, 2, "LNB", 3, "LNE")

            if ph("D5"):
                for sl in range(3):
                    n_sl = len([c for c in range(NCH) if c % 3 == sl])
                    v.wait_ge(dX1Ss[sl], 16 * n_sl)
                for c in range(NCH):
                    tmp = YT[:, 0:T]
                    v.tensor_sub(tmp, A2[:, c * T:(c + 1) * T], BCA[:, :])
                    v.tensor_mul(A2[:, c * T:(c + 1) * T], tmp, BCB[:, :]).then_inc(s["eH"], 1)
                v.wait_ge(s["eGMc"], NCH)
                for hh in range(2):
                    vbar(v.tensor_copy(R[0:1, R_ABS + hh * 512:R_ABS + (hh + 1) * 512],
                                       P[4 + hh][0:1, :]))
                cp = None
                for hh in range(2):
                    cp = v.tensor_copy(R3[:, hh * 512:(hh + 1) * 512], P[6 + hh][0:3, :])
                cp.then_inc(s["eROW"], 1)
                vbar(v.tensor_reduce(SM[0:1, 0:1], R[0:1, R_ABS:R_ABS + VIS_N], AX, ALU.add))
                vbar(v.tensor_reduce(SM[0:1, 1:2], R[0:1, R_ABS + VIS_N:R_ABS + T], AX, ALU.add))
                vbar(v.tensor_add(SM[0:1, 2:3], SM[0:1, 0:1], SM[0:1, 1:2]))
                v.wait_ge(s["eGT"], 1)
                for tt in range(8):
                    vbar(v.tensor_copy(GT[:, tt * 4:tt * 4 + 3], Pb[4][:, tt * 4:tt * 4 + 3]))
                vbar(v.memset(CONTR[:, :], 0.0))
                vbar(v.tensor_mul(SM[0:1, 4:5], SM[0:1, 0:1], MMSK[0:1, 0:1]))
                vbar(v.tensor_mul(SM[0:1, 5:6], SM[0:1, 1:2], MMSK[0:1, 1:2]))
                vbar(v.tensor_mul(SM[0:1, 6:7], SM[0:1, 2:3], MMSK[0:1, 2:3]))
                vbar(v.tensor_add(SM[0:1, 5:6], SM[0:1, 5:6], SM[0:1, 6:7]))
                vbar(v.tensor_copy(CONTR[0:1, 0:1], SM[0:1, 4:5]))
                vbar(v.tensor_copy(CONTR[0:1, 1:2], SM[0:1, 5:6]))
                cp = v.tensor_mul(CONTR[0:1, 2:5], GT[0:1, 0:3], MMSK[0:1, 3:6])
                cp.then_inc(s["eMSK"], 1)
                v.wait_ge(s["eSL"], 16)
                vbar(v.tensor_scalar(SM[0:1, 2:3], STAT[0:1, 0:1], 1.0 / (VIS_N * H), None, ALU.mult))
                vbar(v.tensor_scalar(SM[0:1, 3:4], STAT[0:1, 1:2], 1.0 / (TEX_N * H), None, ALU.mult))
                vbar(v.tensor_scalar(SM[0:1, 2:3], SM[0:1, 2:3], 1e-8, None, ALU.add))
                vbar(v.reciprocal(SM[0:1, 4:5], SM[0:1, 2:3]))
                vbar(v.tensor_mul(SM[0:1, 5:6], SM[0:1, 3:4], SM[0:1, 4:5]))
                vbar(v.tensor_scalar(SM[0:1, 8:11], GLGB[0:1, 0:3], SM[0:1, 5:6], None, ALU.mult))
                vbar(v.tensor_add(SM[0:1, 8:11], SM[0:1, 8:11], GLGB[0:1, 3:6]))
                vbar(v.tensor_add(SM[0:1, 12:15], STAT[0:1, 2:5], SM[0:1, 8:11]))
                vbar(v.tensor_reduce(SM[0:1, 32:33], SM[0:1, 12:15], AX, ALU.max))
                vbar(v.tensor_reduce(SM[0:1, 33:34], SM[0:1, 12:15], AX, ALU.min))
                vbar(v.tensor_scalar(SM[0:1, 16:19], SM[0:1, 12:15], SM[0:1, 32:33], None, ALU.is_ge))
                vbar(v.tensor_scalar(SM[0:1, 24:27], SM[0:1, 12:15], SM[0:1, 32:33], None, ALU.is_lt))
                vbar(v.tensor_scalar(SM[0:1, 28:31], SM[0:1, 12:15], SM[0:1, 33:34], None, ALU.is_gt))
                cp = v.tensor_mul(SM[0:1, 24:27], SM[0:1, 24:27], SM[0:1, 28:31])
                cp.then_inc(s["eB3"], 1)
                v.wait_ge(s["eB3E"], 1)
                vbar(v.tensor_copy(SM3[:, 0:3], P[5][:, 0:3]))
                vbar(v.tensor_copy(SM3[:, 4:7], P[6][:, 0:3]))
                vbar(v.tensor_copy(SM3[:, 8:11], P[7][:, 0:3]))
                cp = None
                for tt in range(8):
                    vbar(v.tensor_add(GT2[:, tt * 4:tt * 4 + 3], GT[:, tt * 4:tt * 4 + 3],
                                      SM3[:, 0:3]))
                    cp = v.tensor_reduce(NM[:, tt:tt + 1], GT2[:, tt * 4:tt * 4 + 3], AX,
                                         ALU.max, negate=True)
                cp.then_inc(s["eGP"], 1)
                v.wait_ge(s["eGE"], 1)
                cp = None
                for tt in range(8):
                    ge = GE[:, tt * 4:tt * 4 + 3]
                    vbar(v.tensor_reduce(GSC[:, 0:1], ge, AX, ALU.add))
                    vbar(v.reciprocal(GSC[:, 1:2], GSC[:, 0:1]))
                    vbar(v.tensor_scalar(GT2[:, tt * 4:tt * 4 + 3], ge, GSC[:, 1:2], None, ALU.mult))
                    gs = GT2[:, tt * 4:tt * 4 + 3]
                    vbar(v.tensor_reduce(GSC[:, 2:3], gs, AX, ALU.max))
                    vbar(v.tensor_reduce(GSC[:, 3:4], gs, AX, ALU.min))
                    vbar(v.tensor_scalar(GSC[:, 4:5], GSC[:, 3:4], -1.0, 1.0 + 1e-8,
                                         ALU.mult, ALU.add))
                    vbar(v.reciprocal(GSC[:, 5:6], GSC[:, 4:5]))
                    vbar(v.tensor_mul(GSC[:, 6:7], GSC[:, 2:3], GSC[:, 5:6]))
                    vbar(v.tensor_sub(GSC[:, 7:8], GSC[:, 4:5], GSC[:, 2:3]))
                    vbar(v.tensor_scalar(GSC[:, 7:8], GSC[:, 7:8], -1e-8, None, ALU.add))
                    vbar(v.tensor_mul(GSC[:, 8:9], GSC[:, 7:8], GSC[:, 5:6]))
                    vbar(v.tensor_scalar(GSC[:, 9:12], SM3[:, 4:7], GSC[:, 6:7], None, ALU.mult))
                    vbar(v.tensor_scalar(GSC[:, 12:15], SM3[:, 8:11], GSC[:, 8:9], None, ALU.mult))
                    vbar(v.tensor_add(GSC[:, 9:12], GSC[:, 9:12], GSC[:, 12:15]))
                    for e in range(3):
                        cp = v.tensor_copy(CTB[:, e * 8 + tt:e * 8 + tt + 1],
                                           GSC[:, 9 + e:10 + e])
                cp.then_inc(s["eCT"], 1)
                v.wait_ge(s["eGTE"], 1)
                v.tensor_copy(CTT[:, :], Pb[5][0:24, 0:128]).then_inc(s["eCTE"], 1)
                for e in range(3):
                    v.wait_ge(s["eCB"], e + 1)
                    cp = None
                    for hh in range(2):
                        cp = v.tensor_copy(CEB[:, e * T + hh * 512:e * T + (hh + 1) * 512],
                                           P[6 + hh][:, :])
                    cp.then_inc(s["eCBE"], 1)

            if ph("E"):
                for e in range(3):
                    for fo in range(NCH):
                        i = e * NCH + fo
                        v.wait_ge(s["fM2"], i + 1)
                        if i >= 2:
                            v.wait_ge(fASs[i % 2], 16 * ((i - 2) // 2 + 1))
                        if e == 2 and fo >= 2:
                            v.wait_ge(s["fST4c"], fo - 1)
                        pb = 4 if i % 2 == 0 else 6
                        dst = ES[:, (fo % 2) * T:(fo % 2 + 1) * T]
                        b2c = B2T[:, e * NCH + fo:e * NCH + fo + 1]
                        cp = None
                        for hh in range(2):
                            cp = v.scalar_tensor_tensor(
                                dst[:, hh * 512:(hh + 1) * 512],
                                P[pb + hh][:, :],
                                b2c, CEB[:, e * T + hh * 512:e * T + (hh + 1) * 512],
                                ALU.add, ALU.mult)
                        if e > 0:
                            ip = (e - 1) * NCH + fo
                            v.wait_ge(fALs[fo % 2], 16 * (ip // 2 + 1))
                            for hh in range(2):
                                cp = v.tensor_add(
                                    dst[:, hh * 512:(hh + 1) * 512],
                                    dst[:, hh * 512:(hh + 1) * 512],
                                    ACS[:, (fo % 2) * T + hh * 512:(fo % 2) * T + (hh + 1) * 512])
                        cp.then_inc(s["fAEV"], 1)

            if ph("F"):
                rows(0, 2, "fST4c", NCH, "LNR")
                bc_evac(0, 2, "LNB", 4, "LNE")
                for c in range(NCH):
                    v.wait_ge(gAs[c % 2], 16 * (c // 2 + 1))
                    v.wait_ge(gX1s[c % 2], 16 * (c // 2 + 1))
                    acc = ACS[:, (c % 2) * T:(c % 2 + 1) * T]
                    x1c = YT[:, (c % 2) * T:(c % 2 + 1) * T]
                    tmp = XS[:, 0:T]
                    v.tensor_sub(tmp, acc, BCA[:, :])
                    v.tensor_mul(tmp, tmp, BCB[:, :])
                    v.tensor_add(x1c, tmp, x1c).then_inc(s["gY"], 1)

    return nc, dbg


def _restage(d):
    x = np.ascontiguousarray(np.asarray(d["L0_attn_x"], dtype=np.float32))
    qkvw = np.asarray(d["L0_attn_attn_qkvw"], dtype=np.float32)
    qkvb = np.asarray(d["L0_attn_attn_qkvb"], dtype=np.float32)
    ow = np.asarray(d["L0_attn_attn_ow"], dtype=np.float32)
    ob = np.asarray(d["L0_attn_attn_ob"], dtype=np.float32)
    gw = np.asarray(d["L0_ffn_gate_w"], dtype=np.float32)
    gb = np.asarray(d["L0_ffn_gate_b"], dtype=np.float32)
    w1 = np.asarray(d["L0_ffn_w1"], dtype=np.float32)
    b1 = np.asarray(d["L0_ffn_b1"], dtype=np.float32)
    w2 = np.asarray(d["L0_ffn_w2"], dtype=np.float32)
    b2 = np.asarray(d["L0_ffn_b2"], dtype=np.float32)

    shared = {
        "qkvw_t": np.ascontiguousarray(
            qkvw.reshape(32, 128, 96, 128).transpose(2, 1, 0, 3)).reshape(96, 128, H),
        "ow_t": np.ascontiguousarray(
            ow.reshape(32, 128, 32, 128).transpose(2, 1, 0, 3)).reshape(32, 128, H),
        "w1_t": np.ascontiguousarray(
            w1.reshape(3, 32, 128, 32, 128).transpose(0, 3, 2, 1, 4)).reshape(3, 32, 128, H),
        "w2_t": np.ascontiguousarray(
            w2.reshape(3, 32, 128, 32, 128).transpose(0, 3, 2, 1, 4)).reshape(3, 32, 128, H),
        "qkvb_t": np.ascontiguousarray(qkvb.reshape(96, 128).T),
        "ob_t": np.ascontiguousarray(ob.reshape(32, 128).T),
        "b1_t": np.ascontiguousarray(
            b1.reshape(3, 32, 128).transpose(2, 0, 1).reshape(128, 96)),
        "b2_t": np.ascontiguousarray(
            b2.reshape(3, 32, 128).transpose(2, 0, 1).reshape(128, 96)),
        "gww_t": np.ascontiguousarray(
            gw[0:H].reshape(32, 128, 3).transpose(1, 0, 2).reshape(128, 96)),
        "gl_gb": np.concatenate([gw[H], gb, np.zeros(2, np.float32)]
                                ).reshape(1, 8).astype(np.float32),
        "ident_in": np.eye(128, dtype=np.float32),
    }
    in_maps = []
    for c in range(8):
        b, hf = c // 2, c % 2
        m = dict(shared)
        m["x_in"] = np.ascontiguousarray(x[b, hf * T:(hf + 1) * T, :].T)
        ev = 1.0 if hf == 0 else 0.0
        m["mmask_in"] = np.array([[ev, ev, 1.0 - ev, ev, ev, ev, 0, 0]], np.float32)
        in_maps.append(m)
    return in_maps


def kernel(**inputs):
    nc, _ = build(upto="F", debug=False)
    in_maps = _restage(inputs)
    res = run_bass_kernel_spmd(nc, in_maps, core_ids=list(range(8)))
    out = np.empty((4, 2048, H), dtype=np.float32)
    for c in range(8):
        b, hf = c // 2, c % 2
        out[b, hf * T:(hf + 1) * T, :] = np.asarray(res.results[c]["y_out"]).T
    return out
